# version 06
# baseline (speedup 1.0000x reference)
import contextlib
import math

import numpy as np
import ml_dtypes

import concourse.bass as bass
import concourse.mybir as mybir
from concourse.bass_utils import run_bass_kernel_spmd

F32 = mybir.dt.float32
BF16 = mybir.dt.bfloat16
U8 = mybir.dt.uint8
I32 = mybir.dt.int32
AF = mybir.ActivationFunctionType
ALU = mybir.AluOpType
AX = mybir.AxisListType

BF = ml_dtypes.bfloat16

T = 2048
D = 1024
NT = T // 128
NCH = D // 128
NQ = 33
KF = NQ * 128
HKF = KF // 2
FD = 4096
NJ = FD // 128
NB = 4
B = T // NB
SIN_SCALE = -2.0 * math.pi / 4096.0

_CACHE = {}


def _host_consts():
    if _CACHE:
        return _CACHE
    import jax
    import jax.numpy as jnp
    cpu = jax.devices('cpu')[0]
    with jax.default_device(cpu):
        pos = jnp.arange(T, dtype=jnp.float32)[:, None]
        c64 = -(math.log(10000.0) / 64.0)
        div = jnp.exp(jnp.arange(0, 64, 2, dtype=jnp.float32) * np.float32(c64))
        prod = pos * div
        pe = np.zeros((T, 64), np.float32)
        pe[:, 0::2] = np.asarray(jnp.sin(prod))
        pe[:, 1::2] = np.asarray(jnp.cos(prod))
        tt = jnp.arange(T, dtype=jnp.float32)
        dec = np.asarray(jnp.exp(np.float32(-0.01) * tt))
    _CACHE['peT'] = np.ascontiguousarray(pe.T)
    _CACHE['dec'] = np.ascontiguousarray(dec.reshape(NT, 128).T)
    coef = np.full((128, NQ), 2.0 / 4096.0, np.float32)
    coef[0, 0] = 1.0 / 4096.0
    coef[0, 32] = 1.0 / 4096.0
    coef[1:, 32] = 0.0
    _CACHE['invcoef'] = coef
    _CACHE['eye'] = np.eye(128, dtype=np.float32).astype(BF)
    return _CACHE


def build_nc(debug=False, max_phase=99):
    nc = bass.Bass("TRN2", target_bir_lowering=False, detect_race_conditions=False)

    x_d = nc.dram_tensor("x", [T, D], F32, kind="ExternalInput")
    Wu_d = nc.dram_tensor("Wu", [D, D], F32, kind="ExternalInput")
    Wv_d = nc.dram_tensor("Wv", [D, D], F32, kind="ExternalInput")
    Wo_d = nc.dram_tensor("Wo", [D, D], F32, kind="ExternalInput")
    bu_d = nc.dram_tensor("bu", [1, D], F32, kind="ExternalInput")
    bv_d = nc.dram_tensor("bv", [1, D], F32, kind="ExternalInput")
    bo_d = nc.dram_tensor("bo", [1, D], F32, kind="ExternalInput")
    W1_d = nc.dram_tensor("W1", [64, 64], F32, kind="ExternalInput")
    W2_d = nc.dram_tensor("W2", [64, 64], F32, kind="ExternalInput")
    W3_d = nc.dram_tensor("W3", [64, D], F32, kind="ExternalInput")
    b1_d = nc.dram_tensor("b1", [64, 1], F32, kind="ExternalInput")
    b2_d = nc.dram_tensor("b2", [64, 1], F32, kind="ExternalInput")
    b3_d = nc.dram_tensor("b3", [1, D], F32, kind="ExternalInput")
    k0a_d = nc.dram_tensor("k0a_b", [4, NCH, 128, 1024], U8, kind="ExternalInput")
    k0b_d = nc.dram_tensor("k0b_b", [4, NCH, 128, 1024], U8, kind="ExternalInput")
    k1_d = nc.dram_tensor("k1_b", [4, NJ, 128, 1024], U8, kind="ExternalInput")
    k2_d = nc.dram_tensor("k2_b", [1, NJ, 128, 1024], U8, kind="ExternalInput")
    sc_d = nc.dram_tensor("sc", [128, 3], F32, kind="ExternalInput")
    peT_d = nc.dram_tensor("peT", [64, T], F32, kind="ExternalInput")
    dec_d = nc.dram_tensor("dec", [128, NT], F32, kind="ExternalInput")
    invc_d = nc.dram_tensor("invcoef", [128, NQ], F32, kind="ExternalInput")
    eye_d = nc.dram_tensor("eye", [128, 128], BF16, kind="ExternalInput")

    out_d = nc.dram_tensor("out", [T, D], F32, kind="ExternalOutput")
    fwdt_d = nc.dram_tensor("fwdt", [NT, 128, KF], BF16, kind="Internal")
    invt_d = nc.dram_tensor("invt", [NQ, 128, T], BF16, kind="Internal")
    m0_d = nc.dram_tensor("m0bf", [4, NCH, 128, 1024], BF16, kind="Internal")
    k1bf_d = nc.dram_tensor("k1bf", [4, NJ, 128, 1024], BF16, kind="Internal")
    k2bf_d = nc.dram_tensor("k2bf", [1, NJ, 128, 1024], BF16, kind="Internal")

    dbg = {}
    if debug:
        for nm, shp, dt in [("dbg_xnT", [128, NCH, T], BF16),
                            ("dbg_u", [128, NT, D], BF16),
                            ("dbg_v", [128, NT, D], BF16),
                            ("dbg_G", [65, KF], BF16),
                            ("dbg_Y", [128, NQ, D], BF16),
                            ("dbg_gated", [128, NT, D], BF16),
                            ("dbg_s", [128, NT, D], BF16),
                            ("dbg_Faug", [128, NT, 65], BF16),
                            ("dbg_h0", [128, NJ, B], BF16),
                            ("dbg_h1", [128, NJ, B], BF16),
                            ("dbg_gT", [128, NCH, B], BF16)]:
            dbg[nm] = nc.dram_tensor(nm, shp, dt, kind="ExternalOutput")
        dbg["dbg_m0"] = nc.dram_tensor("dbg_m0", [4, NCH, 128, 1024], BF16, kind="ExternalOutput")
        dbg["dbg_k1b"] = nc.dram_tensor("dbg_k1b", [4, NJ, 128, 1024], BF16, kind="ExternalOutput")
        dbg["dbg_k2b"] = nc.dram_tensor("dbg_k2b", [1, NJ, 128, 1024], BF16, kind="ExternalOutput")

    es = contextlib.ExitStack()

    def sb(name, shape, dtype):
        return es.enter_context(nc.sbuf_tensor(name, shape, dtype))

    xnT = sb("xnT", [128, NCH, T], BF16)
    sbf = sb("sbf", [128, NT, D], BF16)
    dec_s = sb("dec_s", [128, NT], F32)
    invc_s = sb("invc_s", [128, NQ], F32)
    eye_s = sb("eye_s", [128, 128], BF16)
    sc2 = sb("sc2", [128, 3], F32)
    bu_bf = sb("bu_bf", [1, D], BF16)
    bv_bf = sb("bv_bf", [1, D], BF16)
    bo_bf = sb("bo_bf", [1, D], BF16)
    ones_r = sb("ones_r", [1, 128], BF16)
    W3aug = sb("W3aug", [65, D], BF16)
    Faug = sb("Faug", [128, NT, 65], BF16)
    kvals = sb("kvals", [128, NQ], F32)
    tpall = sb("tpall", [128, NT], F32)
    lnA = sb("lnA", [128, NT], F32)
    lnB = sb("lnB", [128, NT], F32)
    es_mid = contextlib.ExitStack()
    G1 = es_mid.enter_context(nc.sbuf_tensor("G1", [65, KF], BF16))
    vbuf = es_mid.enter_context(nc.sbuf_tensor("vbuf", [128, NT, D], BF16))

    s_ld = es.enter_context(nc.semaphore("s_ld"))
    s_st = es.enter_context(nc.semaphore("s_st"))
    s_t = es.enter_context(nc.semaphore("s_t"))
    s_v = es.enter_context(nc.semaphore("s_v"))
    s_s = es.enter_context(nc.semaphore("s_s"))
    s_w = es.enter_context(nc.semaphore("s_w"))

    n = dict(ld=0, st=0, t=0, v=0, s=0, w=0)

    def finish():
        with nc.Block() as blk:
            @blk.gpsimd
            def _(g):
                g.wait_ge(s_st, 16 * n['st'])

            @blk.sync
            def _(sy):
                sy.wait_ge(s_ld, 16 * n['ld'])
        es_mid.close()
        es.close()
        return nc

    ps = contextlib.ExitStack()
    peT_f = ps.enter_context(nc.sbuf_tensor("peT_f", [64, T], F32))
    peT_bf = ps.enter_context(nc.sbuf_tensor("peT_bf", [64, T], BF16))
    W1f = ps.enter_context(nc.sbuf_tensor("W1f", [64, 64], F32))
    W2f = ps.enter_context(nc.sbuf_tensor("W2f", [64, 64], F32))
    W3af = ps.enter_context(nc.sbuf_tensor("W3af", [65, D], F32))
    bxf = ps.enter_context(nc.sbuf_tensor("bxf", [1, 3 * D], F32))
    W1bf = ps.enter_context(nc.sbuf_tensor("W1bf", [64, 64], BF16))
    W2bf = ps.enter_context(nc.sbuf_tensor("W2bf", [64, 64], BF16))
    b1s = ps.enter_context(nc.sbuf_tensor("b1s", [64, 1], F32))
    b2s = ps.enter_context(nc.sbuf_tensor("b2s", [64, 1], F32))
    scf = ps.enter_context(nc.sbuf_tensor("scf", [128, 3], F32))
    F1g = ps.enter_context(nc.sbuf_tensor("F1g", [64, T], BF16))
    F2g = ps.enter_context(nc.sbuf_tensor("F2g", [128, T], BF16))
    xch = [ps.enter_context(nc.sbuf_tensor(f"xch{i}", [128, D], F32)) for i in range(2)]
    xnch = [ps.enter_context(nc.sbuf_tensor(f"xnch{i}", [128, D], BF16)) for i in range(2)]
    sqs = ps.enter_context(nc.sbuf_tensor("sqs", [128, D], F32))
    sums = ps.enter_context(nc.sbuf_tensor("sums", [128, NT], F32))
    sumsq = ps.enter_context(nc.sbuf_tensor("sumsq", [128, NT], F32))
    lw = [ps.enter_context(nc.sbuf_tensor(f"lw{i}", [128, NT], F32)) for i in range(3)]
    psF = ps.enter_context(nc.psum_tensor("psF", [64, T], F32))
    psTr = [ps.enter_context(nc.psum_tensor(f"psTr{i}", [128, 128], BF16)) for i in range(4)]

    with nc.Block() as blk:
        @blk.sync
        def _(sy):
            for ap_s, ap_d in [
                (peT_f[:], peT_d[:, :]), (W1f[:], W1_d[:, :]), (W2f[:], W2_d[:, :]),
                (W3af[0:64, :], W3_d[:, :]), (W3af[64:65, :], b3_d[:, :]),
                (b1s[:], b1_d[:, :]), (b2s[:], b2_d[:, :]),
                (bxf[:, 0:D], bu_d[:, :]), (bxf[:, D:2 * D], bv_d[:, :]),
                (bxf[:, 2 * D:3 * D], bo_d[:, :]),
                (dec_s[:], dec_d[:, :]), (invc_s[:], invc_d[:, :]),
                (eye_s[:], eye_d[:, :]), (scf[:], sc_d[:, :]),
            ]:
                sy.dma_start(ap_s, ap_d).then_inc(s_ld, 16)
                n['ld'] += 1

        @blk.scalar
        def _(s):
            s.wait_ge(s_ld, 16 * n['ld'])
            s.activation(peT_bf[:], peT_f[:], AF.Copy)
            s.activation(W1bf[:], W1f[:], AF.Copy)
            s.activation(W2bf[:], W2f[:], AF.Copy)
            s.activation(W3aug[:], W3af[:], AF.Copy)
            s.activation(bu_bf[:], bxf[:, 0:D], AF.Copy)
            s.activation(bv_bf[:], bxf[:, D:2 * D], AF.Copy)
            s.activation(bo_bf[:], bxf[:, 2 * D:3 * D], AF.Copy)

        @blk.vector
        def _(v):
            v.wait_ge(s_ld, 16 * n['ld'])
            v.memset(ones_r[:], 1.0)
            v.tensor_scalar_mul(sc2[:], scf[:], 2.0)
            for j in range(NT):
                v.memset(Faug[:, j, 64:65], 1.0)
            v.memset(F2g[64:128, :], 0.0)

        @blk.gpsimd
        def _(g):
            g.iota(tpall[:], [[128, NT]], base=0, channel_multiplier=1,
                   allow_small_or_imprecise_dtypes=True)
            g.iota(kvals[:, 0:32], [[128, 16], [0, 2]], base=0, channel_multiplier=1,
                   allow_small_or_imprecise_dtypes=True)
            g.iota(kvals[:, 32:33], [[0, 1]], base=2048, channel_multiplier=1,
                   allow_small_or_imprecise_dtypes=True)

    if max_phase < 0.2:
        ps.close()
        return finish()

    with nc.Block() as blk:
        @blk.tensor
        def _(t):
            for i in range(T // 512):
                ins = t.matmul(psF[:, 512 * i:512 * (i + 1)], W1bf[:],
                               peT_bf[:, 512 * i:512 * (i + 1)], start=True, stop=True)
            ins.then_inc(s_t, 1)
            n['t'] += 1

        @blk.scalar
        def _(s):
            s.wait_ge(s_t, n['t'])
            s.activation(F1g[:], psF[:], AF.Gelu_apprx_tanh, bias=b1s[:])

    with nc.Block() as blk:
        @blk.tensor
        def _(t):
            for i in range(T // 512):
                ins = t.matmul(psF[:, 512 * i:512 * (i + 1)], W2bf[:],
                               F1g[:, 512 * i:512 * (i + 1)], start=True, stop=True)
            ins.then_inc(s_t, 1)
            n['t'] += 1

        @blk.scalar
        def _(s):
            s.wait_ge(s_t, n['t'])
            s.activation(F2g[0:64, :], psF[:], AF.Gelu_apprx_tanh, bias=b2s[:])

    if max_phase < 0.4:
        ps.close()
        return finish()

    with nc.Block() as blk:
        T0, V0 = n['t'], n['v']

        @blk.tensor
        def _(t):
            for j in range(NT + 2):
                if j >= 4:
                    t.wait_ge(s_v, V0 + j - 3)
                src_ap = F2g[:, 128 * j:128 * (j + 1)] if j < NT else eye_s[:]
                t.transpose(psTr[j % 4][:], src_ap, eye_s[:]).then_inc(s_t, 1)
                n['t'] += 1

        @blk.vector
        def _(v):
            for j in range(NT):
                v.wait_ge(s_t, T0 + j + 3)
                v.tensor_copy(Faug[:, j, 0:64], psTr[j % 4][:, 0:64]).then_inc(s_v, 1)
                n['v'] += 1

    if max_phase < 0.45:
        ps.close()
        return finish()

    with nc.Block() as blk:
        L0_, S0 = n['ld'], n['s']

        @blk.sync
        def _(sy):
            for j in range(NT):
                if j >= 2:
                    sy.wait_ge(s_s, S0 + 2 * (j - 1))
                sy.dma_start(xch[j % 2][:], x_d[128 * j:128 * (j + 1), :]).then_inc(s_ld, 16)
                n['ld'] += 1

        @blk.scalar
        def _(s):
            for j in range(NT):
                s.wait_ge(s_ld, 16 * (L0_ + j + 1))
                s.activation(sqs[:], xch[j % 2][:], AF.Identity,
                             accum_out=sums[:, j:j + 1]).then_inc(s_s, 1)
                s.activation(sqs[:], xch[j % 2][:], AF.Square,
                             accum_out=sumsq[:, j:j + 1]).then_inc(s_s, 1)
                n['s'] += 2

    with nc.Block() as blk:
        V0, S0 = n['v'], n['s']

        @blk.vector
        def _(v):
            v.tensor_scalar_mul(lw[0][:], sums[:], -1.0 / D)
            v.tensor_scalar(lw[1][:], sumsq[:], 1.0 / D, 1e-5,
                            ALU.mult, ALU.add).then_inc(s_v, 1)
            n['v'] += 1
            v.wait_ge(s_v, V0 + 1)
            v.tensor_tensor(lw[2][:], lw[0][:], lw[0][:], ALU.mult).then_inc(s_v, 1)
            n['v'] += 1
            v.wait_ge(s_v, V0 + 2)
            v.tensor_sub(lw[1][:], lw[1][:], lw[2][:]).then_inc(s_v, 1)
            n['v'] += 1

        @blk.scalar
        def _(s):
            s.wait_ge(s_v, V0 + 3)
            s.activation(lw[2][:], lw[1][:], AF.Sqrt).then_inc(s_s, 1)
            n['s'] += 1

        @blk.vector
        def _(v):
            v.wait_ge(s_s, S0 + 1)
            v.reciprocal(lnA[:], lw[2][:]).then_inc(s_v, 1)
            n['v'] += 1
            v.wait_ge(s_v, V0 + 4)
            v.tensor_tensor(lnB[:], lw[0][:], lnA[:], ALU.mult).then_inc(s_v, 1)
            n['v'] += 1

    if max_phase < 0.48:
        ps.close()
        return finish()

    with nc.Block() as blk:
        L0_, S0, T0, W0 = n['ld'], n['s'], n['t'], n['w']

        @blk.sync
        def _(sy):
            for j in range(NT):
                if j >= 2:
                    sy.wait_ge(s_s, S0 + j - 1)
                sy.dma_start(xch[j % 2][:], x_d[128 * j:128 * (j + 1), :]).then_inc(s_ld, 16)
                n['ld'] += 1

        @blk.scalar
        def _(s):
            for j in range(NT):
                s.wait_ge(s_ld, 16 * (L0_ + j + 1))
                if j >= 1:
                    s.wait_ge(s_t, T0 + NCH * (j - 1))
                s.activation(xnch[j % 2][:], xch[j % 2][:], AF.Identity,
                             scale=lnA[:, j:j + 1], bias=lnB[:, j:j + 1]).then_inc(s_s, 1)
                n['s'] += 1
            s.activation(sqs[:], xch[(NT - 1) % 2][:], AF.Square).then_inc(s_s, 1)
            n['s'] += 1

        @blk.tensor
        def _(t):
            k = 0
            for j in range(NT):
                t.wait_ge(s_s, S0 + min(j + 2, NT + 1))
                for cc in range(NCH):
                    if k >= 4:
                        t.wait_ge(s_w, W0 + k - 3)
                    t.transpose(psTr[k % 4][:], xnch[j % 2][:, 128 * cc:128 * (cc + 1)],
                                eye_s[:]).then_inc(s_t, 1)
                    n['t'] += 1
                    k += 1
            for d in range(2):
                t.wait_ge(s_w, W0 + k - 3)
                t.transpose(psTr[k % 4][:], eye_s[:], eye_s[:]).then_inc(s_t, 1)
                n['t'] += 1
                k += 1

        @blk.vector
        def _(v):
            k = 0
            for j in range(NT):
                for cc in range(NCH):
                    v.wait_ge(s_t, T0 + k + 3)
                    v.tensor_copy(xnT[:, cc, 128 * j:128 * (j + 1)],
                                  psTr[k % 4][:]).then_inc(s_w, 1)
                    n['w'] += 1
                    k += 1

    ps.close()
    if max_phase < 1:
        return finish()

    with contextlib.ExitStack() as ps1:
        krow = ps1.enter_context(nc.sbuf_tensor("krow", [128, KF], F32))
        trow = ps1.enter_context(nc.sbuf_tensor("trow", [128, T], F32))
        ktf = ps1.enter_context(nc.sbuf_tensor("ktf", [128, KF], F32))
        kti = ps1.enter_context(nc.sbuf_tensor("kti", [128, KF], I32))
        ktm = ps1.enter_context(nc.sbuf_tensor("ktm", [128, KF], F32))
        stb = [ps1.enter_context(nc.sbuf_tensor(f"stb{i}", [128, KF], BF16)) for i in range(2)]

        with nc.Block() as blk:
            @blk.gpsimd
            def _(g):
                g.iota(krow[:, 0:4096], [[128, 16], [0, 2], [1, 128]], base=0,
                       channel_multiplier=0, allow_small_or_imprecise_dtypes=True)
                g.iota(krow[:, 4096:KF], [[1, 128]], base=2048,
                       channel_multiplier=0, allow_small_or_imprecise_dtypes=True)
                g.iota(trow[:], [[1, T]], base=0, channel_multiplier=0,
                       allow_small_or_imprecise_dtypes=True)

        with nc.Block() as blk:
            V0, S0, ST0 = n['v'], n['s'], n['st']
            kre = ktf[:, 0:4096].rearrange("p (m two f) -> p m two f", two=2, f=128)
            krw = krow[:, 0:4096].rearrange("p (m two f) -> p m two f", two=2, f=128)

            @blk.vector
            def _(v):
                for j in range(NT):
                    v.tensor_scalar(kre[:, :, 0, :], krw[:, :, 0, :], tpall[:, j:j + 1],
                                    1024.0, ALU.mult, ALU.add)
                    v.tensor_scalar(kre[:, :, 1, :], krw[:, :, 1, :], tpall[:, j:j + 1],
                                    2048.0, ALU.mult, ALU.add)
                    v.tensor_scalar(ktf[:, 4096:KF], krow[:, 4096:KF], tpall[:, j:j + 1],
                                    1024.0, ALU.mult, ALU.add)
                    v.tensor_copy(kti[:], ktf[:])
                    v.tensor_single_scalar(kti[:], kti[:], 4095, ALU.bitwise_and)
                    if j >= 1:
                        v.wait_ge(s_s, S0 + j)
                    v.tensor_scalar(ktm[:], kti[:], -2048.0, None, ALU.add).then_inc(s_v, 1)
                    n['v'] += 1

            @blk.scalar
            def _(s):
                for j in range(NT):
                    s.wait_ge(s_v, V0 + j + 1)
                    if j >= 2:
                        s.wait_ge(s_st, 16 * (ST0 + j - 1))
                    s.activation(stb[j % 2][:], ktm[:], AF.Sin,
                                 scale=SIN_SCALE).then_inc(s_s, 1)
                    n['s'] += 1

            @blk.gpsimd
            def _(g):
                for j in range(NT):
                    g.wait_ge(s_s, S0 + j + 1)
                    g.dma_start(fwdt_d[j, :, :], stb[j % 2][:]).then_inc(s_st, 16)
                    n['st'] += 1

        with nc.Block() as blk:
            V0, S0, ST0 = n['v'], n['s'], n['st']

            @blk.vector
            def _(v):
                for q in range(NQ):
                    cq = 1024.0 if (q % 2 == 0 or q == 32) else 2048.0
                    v.tensor_scalar(ktf[:, 0:T], trow[:], kvals[:, q:q + 1],
                                    cq, ALU.mult, ALU.add)
                    v.tensor_copy(kti[:, 0:T], ktf[:, 0:T])
                    v.tensor_single_scalar(kti[:, 0:T], kti[:, 0:T], 4095, ALU.bitwise_and)
                    if q >= 1:
                        v.wait_ge(s_s, S0 + q)
                    v.tensor_scalar(ktm[:, 0:T], kti[:, 0:T], -2048.0, None,
                                    ALU.add).then_inc(s_v, 1)
                    n['v'] += 1
                    if q >= 1:
                        if q >= 3:
                            v.wait_ge(s_st, 16 * (ST0 + q - 2))
                        v.tensor_scalar(stb[(q - 1) % 2][:, 0:T], ktf[:, KF - T:KF],
                                        invc_s[:, q - 1:q], None, ALU.mult).then_inc(s_v, 1)
                        n['v'] += 1
                v.wait_ge(s_s, S0 + NQ)
                v.wait_ge(s_st, 16 * (ST0 + NQ - 2))
                v.tensor_scalar(stb[(NQ - 1) % 2][:, 0:T], ktf[:, KF - T:KF],
                                invc_s[:, NQ - 1:NQ], None, ALU.mult).then_inc(s_v, 1)
                n['v'] += 1

            @blk.scalar
            def _(s):
                for q in range(NQ):
                    s.wait_ge(s_v, V0 + (2 * q + 1 if q >= 1 else 1))
                    s.activation(ktf[:, KF - T:KF], ktm[:, 0:T], AF.Sin,
                                 scale=SIN_SCALE).then_inc(s_s, 1)
                    n['s'] += 1

            @blk.gpsimd
            def _(g):
                for q in range(NQ):
                    coef_inc = 2 * q + 3 if q < NQ - 1 else 2 * NQ
                    g.wait_ge(s_v, V0 + coef_inc)
                    g.dma_start(invt_d[q, :, :], stb[q % 2][:, 0:T]).then_inc(s_st, 16)
                    n['st'] += 1

    if max_phase < 2:
        return finish()

    with contextlib.ExitStack() as ps2:
        dbuf = [ps2.enter_context(nc.sbuf_tensor(f"dbuf{i}", [128, HKF], BF16)) for i in range(2)]
        ddec = [ps2.enter_context(nc.sbuf_tensor(f"ddec{i}", [128, HKF], BF16)) for i in range(2)]
        psG = ps2.enter_context(nc.psum_tensor("psG", [65, HKF], F32))
        with nc.Block() as blk:
            L0_, V0, T0, S0 = n['ld'], n['v'], n['t'], n['s']

            @blk.sync
            def _(sy):
                sy.wait_ge(s_st, 16 * n['st'])
                for h in range(2):
                    for dc in range(NT):
                        k = h * NT + dc
                        if k >= 2:
                            sy.wait_ge(s_v, V0 + k - 1)
                        sy.dma_start(dbuf[k % 2][:],
                                     fwdt_d[dc, :, h * HKF:(h + 1) * HKF]).then_inc(s_ld, 16)
                        n['ld'] += 1

            @blk.vector
            def _(v):
                for h in range(2):
                    for dc in range(NT):
                        k = h * NT + dc
                        v.wait_ge(s_ld, 16 * (L0_ + k + 1))
                        if k >= 2:
                            v.wait_ge(s_t, T0 + k - 1)
                        v.tensor_scalar(ddec[k % 2][:], dbuf[k % 2][:],
                                        dec_s[:, dc:dc + 1], None, ALU.mult).then_inc(s_v, 1)
                        n['v'] += 1

            @blk.tensor
            def _(t):
                for h in range(2):
                    if h >= 1:
                        t.wait_ge(s_s, S0 + h)
                    for dc in range(NT):
                        k = h * NT + dc
                        t.wait_ge(s_v, V0 + k + 1)
                        for i in range(5):
                            c0, c1 = 512 * i, min(512 * (i + 1), HKF)
                            ins = t.matmul(psG[:, c0:c1], Faug[:, dc, :], ddec[k % 2][:, c0:c1],
                                           start=(dc == 0), stop=(dc == NT - 1))
                        ins.then_inc(s_t, 1)
                        n['t'] += 1

            @blk.scalar
            def _(s):
                for h in range(2):
                    s.wait_ge(s_t, T0 + (h + 1) * NT)
                    s.activation(G1[:, h * HKF:(h + 1) * HKF], psG[:], AF.Copy).then_inc(s_s, 1)
                    n['s'] += 1

    if max_phase < 2.5:
        return finish()

    with contextlib.ExitStack() as psk:
        kua = psk.enter_context(nc.sbuf_tensor("kua", [128, 2, 4, 1024], U8))
        kub = psk.enter_context(nc.sbuf_tensor("kub", [128, 2, 4, 1024], U8))
        kwo = psk.enter_context(nc.sbuf_tensor("kwo", [128, 2, 4, 1024], BF16))
        with nc.Block() as blk:
            L0_, V0, S0, ST0 = n['ld'], n['v'], n['s'], n['st']
            jobs = ([(k0a_d, k0b_d, m0_d, g, 4 * rb) for g in range(4) for rb in range(NCH // 4)]
                    + [(k1_d, None, k1bf_d, g, 4 * rb) for g in range(4) for rb in range(NJ // 4)]
                    + [(k2_d, None, k2bf_d, 0, 4 * rb) for rb in range(NJ // 4)])

            @blk.sync
            def _(sy):
                for i, (sa, sb, dst, g, r0) in enumerate(jobs):
                    if i >= 2:
                        sy.wait_ge(s_v, V0 + i - 1)
                    sy.dma_start(kua[:, i % 2, :, :],
                                 sa.ap().transpose([2, 0, 1, 3])[:, g, r0:r0 + 4, :]).then_inc(s_ld, 16)
                    n['ld'] += 1
                    if sb is not None:
                        sy.dma_start(kub[:, i % 2, :, :],
                                     sb.ap().transpose([2, 0, 1, 3])[:, g, r0:r0 + 4, :]).then_inc(s_ld, 16)
                        n['ld'] += 1

            ld_at = []
            acc = L0_
            for (sa, sb, dst, g, r0) in jobs:
                acc += 2 if sb is not None else 1
                ld_at.append(acc)

            @blk.vector
            def _(v):
                for i, (sa, sb, dst, g, r0) in enumerate(jobs):
                    v.wait_ge(s_ld, 16 * ld_at[i])
                    if i >= 2:
                        v.wait_ge(s_st, 16 * (ST0 + i - 1))
                    if sb is not None:
                        v.scalar_tensor_tensor(kwo[:, i % 2, :, :], kua[:, i % 2, :, :], 1.0,
                                               kub[:, i % 2, :, :], ALU.mult,
                                               ALU.subtract).then_inc(s_v, 1)
                    else:
                        v.tensor_scalar(kwo[:, i % 2, :, :], kua[:, i % 2, :, :], 2.0, -1.0,
                                        ALU.mult, ALU.add).then_inc(s_v, 1)
                    n['v'] += 1

            @blk.gpsimd
            def _(g_):
                for i, (sa, sb, dst, g, r0) in enumerate(jobs):
                    g_.wait_ge(s_v, V0 + i + 1)
                    g_.dma_start(dst.ap().transpose([2, 0, 1, 3])[:, g, r0:r0 + 4, :],
                                 kwo[:, i % 2, :, :]).then_inc(s_st, 16)
                    n['st'] += 1

    if max_phase < 3:
        return finish()

    for (W_src, b_bf, obuf, tag) in [(Wu_d, bu_bf, sbf, "u"), (Wv_d, bv_bf, vbuf, "v")]:
        with contextlib.ExitStack() as ps3:
            wf = [ps3.enter_context(nc.sbuf_tensor(f"wf{tag}{i}", [128, D], F32)) for i in range(2)]
            wbf = ps3.enter_context(nc.sbuf_tensor(f"wbf{tag}", [128, NCH, D], BF16))
            psU = [ps3.enter_context(nc.psum_tensor(f"psU{tag}{i}", [128, D], F32)) for i in range(2)]
            with nc.Block() as blk:
                L0_, S0 = n['ld'], n['s']

                @blk.sync
                def _(sy):
                    for cc in range(NCH):
                        if cc >= 2:
                            sy.wait_ge(s_s, S0 + cc - 1)
                        sy.dma_start(wf[cc % 2][:], W_src[128 * cc:128 * (cc + 1), :]).then_inc(s_ld, 16)
                        n['ld'] += 1

                @blk.scalar
                def _(s):
                    for cc in range(NCH):
                        s.wait_ge(s_ld, 16 * (L0_ + cc + 1))
                        s.activation(wbf[:, cc, :], wf[cc % 2][:], AF.Copy).then_inc(s_s, 1)
                        n['s'] += 1

            with nc.Block() as blk:
                T0, S0 = n['t'], n['s']

                @blk.tensor
                def _(t):
                    for j in range(NT):
                        if j >= 2:
                            t.wait_ge(s_s, S0 + j - 1)
                        pp = psU[j % 2]
                        for cc in range(NCH):
                            for i in range(2):
                                t.matmul(pp[:, 512 * i:512 * (i + 1)],
                                         xnT[:, cc, 128 * j:128 * (j + 1)],
                                         wbf[:, cc, 512 * i:512 * (i + 1)],
                                         start=(cc == 0), stop=False)
                        for i in range(2):
                            ins = t.matmul(pp[:, 512 * i:512 * (i + 1)], ones_r[:],
                                           b_bf[:, 512 * i:512 * (i + 1)],
                                           start=False, stop=True)
                        ins.then_inc(s_t, 1)
                        n['t'] += 1

                @blk.scalar
                def _(s):
                    for j in range(NT):
                        s.wait_ge(s_t, T0 + j + 1)
                        s.activation(obuf[:, j, :], psU[j % 2][:], AF.Copy).then_inc(s_s, 1)
                        n['s'] += 1

    if max_phase < 4:
        return finish()

    with contextlib.ExitStack() as psY:
        Yspec = psY.enter_context(nc.sbuf_tensor("Yspec", [128, NQ, D], BF16))

        with contextlib.ExitStack() as ps4:
            ftile = [ps4.enter_context(nc.sbuf_tensor(f"ftile{i}", [128, NT, 128], BF16)) for i in range(3)]
            Hsb = [ps4.enter_context(nc.sbuf_tensor(f"Hsb{i}", [128, D], BF16)) for i in range(2)]
            tp1 = ps4.enter_context(nc.sbuf_tensor("tp1", [128, D], F32))
            tp2 = ps4.enter_context(nc.sbuf_tensor("tp2", [128, D], F32))
            psV = [ps4.enter_context(nc.psum_tensor(f"psV{i}", [128, D], F32)) for i in range(3)]
            psH = ps4.enter_context(nc.psum_tensor("psH", [128, D], F32))
            fsrc = fwdt_d.ap().transpose([1, 0, 2])

            with nc.Block() as blk:
                L0_, T0, S0, V0, ST4_ = n['ld'], n['t'], n['s'], n['v'], n['st']

                @blk.sync
                def _(sy):
                    for q in range(NQ):
                        if q >= 3:
                            sy.wait_ge(s_t, T0 + 2 * (q - 2))
                        for gg in range(2):
                            sy.dma_start(ftile[q % 3][:, 4 * gg:4 * (gg + 1), :],
                                         fsrc[:, 4 * gg:4 * (gg + 1), 128 * q:128 * (q + 1)]).then_inc(s_ld, 16)
                            n['ld'] += 1

                @blk.gpsimd
                def _(g_):
                    for q in range(NQ):
                        if q >= 3:
                            g_.wait_ge(s_t, T0 + 2 * (q - 2))
                        for gg in range(2, 4):
                            g_.dma_start(ftile[q % 3][:, 4 * gg:4 * (gg + 1), :],
                                         fsrc[:, 4 * gg:4 * (gg + 1), 128 * q:128 * (q + 1)]).then_inc(s_st, 16)
                            n['st'] += 1

                @blk.tensor
                def _(t):
                    t.wait_ge(s_ld, 16 * (L0_ + 2))
                    t.wait_ge(s_st, 16 * (ST4_ + 2))
                    t.drain()
                    for q in range(NQ):
                        t.wait_ge(s_ld, 16 * (L0_ + min(2 * (q + 2), 2 * NQ)))
                        t.wait_ge(s_st, 16 * (ST4_ + min(2 * (q + 2), 2 * NQ)))
                        if q >= 3:
                            t.wait_ge(s_v, V0 + 2 * ((q - 3) // 2 + 1))
                        pv = psV[q % 3]
                        for j in range(NT):
                            for i in range(2):
                                ins = t.matmul(pv[:, 512 * i:512 * (i + 1)],
                                               ftile[q % 3][:, j, :],
                                               vbuf[:, j, 512 * i:512 * (i + 1)],
                                               start=(j == 0), stop=(j == NT - 1))
                        ins.then_inc(s_t, 1)
                        n['t'] += 1
                        if q >= 1:
                            t.wait_ge(s_s, S0 + q)
                        for i in range(2):
                            ins = t.matmul(psH[:, 512 * i:512 * (i + 1)],
                                           G1[:, 128 * q:128 * (q + 1)],
                                           W3aug[:, 512 * i:512 * (i + 1)],
                                           start=True, stop=True)
                        ins.then_inc(s_t, 1)
                        n['t'] += 1

                @blk.scalar
                def _(s):
                    for q in range(NQ):
                        s.wait_ge(s_t, T0 + 2 * (q + 1))
                        if q >= 2:
                            s.wait_ge(s_v, V0 + 2 * ((q - 2) // 2 + 1))
                        s.activation(Hsb[q % 2][:], psH[:], AF.Copy).then_inc(s_s, 1)
                        n['s'] += 1

                @blk.vector
                def _(v):
                    for m in range(16):
                        qre, qim = 2 * m, 2 * m + 1
                        v.wait_ge(s_t, T0 + 2 * qim + 1)
                        v.wait_ge(s_s, S0 + qim + 1)
                        re, im = psV[qre % 3], psV[qim % 3]
                        v.tensor_tensor(tp1[:], re[:], Hsb[0][:], ALU.mult)
                        v.tensor_tensor(tp2[:], im[:], Hsb[1][:], ALU.mult)
                        v.tensor_sub(Yspec[:, qre, :], tp1[:], tp2[:]).then_inc(s_v, 1)
                        v.tensor_tensor(tp1[:], re[:], Hsb[1][:], ALU.mult)
                        v.tensor_tensor(tp2[:], im[:], Hsb[0][:], ALU.mult)
                        v.tensor_add(Yspec[:, qim, :], tp1[:], tp2[:]).then_inc(s_v, 1)
                        n['v'] += 2
                    v.wait_ge(s_t, T0 + 2 * 32 + 1)
                    v.wait_ge(s_s, S0 + 33)
                    v.tensor_tensor(Yspec[:, 32, :], psV[32 % 3][:], Hsb[0][:],
                                    ALU.mult).then_inc(s_v, 2)
                    n['v'] += 2

        with contextlib.ExitStack() as ps5:
            itile = [ps5.enter_context(nc.sbuf_tensor(f"itile{i}", [128, NQ, 128], BF16)) for i in range(3)]
            psC = [ps5.enter_context(nc.psum_tensor(f"psC{i}", [128, D], F32)) for i in range(2)]
            isrc = invt_d.ap().transpose([1, 0, 2])
            with nc.Block() as blk:
                L0_, T0, V0, ST5_ = n['ld'], n['t'], n['v'], n['st']

                @blk.sync
                def _(sy):
                    for j in range(NT):
                        if j >= 3:
                            sy.wait_ge(s_t, T0 + j - 2)
                        for gg in range(5):
                            q0, q1 = 4 * gg, min(4 * (gg + 1), NQ)
                            sy.dma_start(itile[j % 3][:, q0:q1, :],
                                         isrc[:, q0:q1, 128 * j:128 * (j + 1)]).then_inc(s_ld, 16)
                            n['ld'] += 1

                @blk.gpsimd
                def _(g_):
                    for j in range(NT):
                        if j >= 3:
                            g_.wait_ge(s_t, T0 + j - 2)
                        for gg in range(5, 9):
                            q0, q1 = 4 * gg, min(4 * (gg + 1), NQ)
                            g_.dma_start(itile[j % 3][:, q0:q1, :],
                                         isrc[:, q0:q1, 128 * j:128 * (j + 1)]).then_inc(s_st, 16)
                            n['st'] += 1

                @blk.tensor
                def _(t):
                    t.wait_ge(s_ld, 16 * (L0_ + 5))
                    t.wait_ge(s_st, 16 * (ST5_ + 4))
                    t.drain()
                    for j in range(NT):
                        t.wait_ge(s_ld, 16 * (L0_ + min(5 * (j + 2), 5 * NT)))
                        t.wait_ge(s_st, 16 * (ST5_ + min(4 * (j + 2), 4 * NT)))
                        if j >= 2:
                            t.wait_ge(s_v, V0 + j - 1)
                        pc = psC[j % 2]
                        for q in range(NQ):
                            for i in range(2):
                                ins = t.matmul(pc[:, 512 * i:512 * (i + 1)],
                                               itile[j % 3][:, q, :],
                                               Yspec[:, q, 512 * i:512 * (i + 1)],
                                               start=(q == 0), stop=(q == NQ - 1))
                        ins.then_inc(s_t, 1)
                        n['t'] += 1

                @blk.vector
                def _(v):
                    for j in range(NT):
                        v.wait_ge(s_t, T0 + j + 1)
                        v.tensor_tensor(vbuf[:, j, :], sbf[:, j, :],
                                        psC[j % 2][:], ALU.mult).then_inc(s_v, 1)
                        n['v'] += 1

        if debug:
            with nc.Block() as blk:
                @blk.gpsimd
                def _(g):
                    for nm, src in [("dbg_xnT", xnT[:, :, :]), ("dbg_G", G1[:, :]),
                                    ("dbg_Faug", Faug[:, :, :]), ("dbg_u", sbf[:, :, :]),
                                    ("dbg_Y", Yspec[:, :, :]), ("dbg_gated", vbuf[:, :, :])]:
                        g.dma_start(dbg[nm].ap(), src).then_inc(s_st, 16)
                        n['st'] += 1
                    for nm, srcd in [("dbg_m0", m0_d), ("dbg_k1b", k1bf_d), ("dbg_k2b", k2bf_d)]:
                        for gi in range(srcd.shape[0]):
                            g.dma_start(dbg[nm].ap()[gi], srcd.ap()[gi]).then_inc(s_st, 16)
                            n['st'] += 1
                    g.wait_ge(s_st, 16 * n['st'])

    if max_phase < 6:
        return finish()

    with contextlib.ExitStack() as ps6:
        wof = [ps6.enter_context(nc.sbuf_tensor(f"wof{i}", [128, D], F32)) for i in range(2)]
        wobf = ps6.enter_context(nc.sbuf_tensor("wobf", [128, NCH, D], BF16))
        gstg = [ps6.enter_context(nc.sbuf_tensor(f"gstg{i}", [128, NCH, 128], BF16)) for i in range(2)]
        glhs = [ps6.enter_context(nc.sbuf_tensor(f"glhs{i}", [128, NCH, 128], BF16)) for i in range(2)]
        psS = [ps6.enter_context(nc.psum_tensor(f"psS{i}", [128, D], F32)) for i in range(2)]
        psT6 = [ps6.enter_context(nc.psum_tensor(f"psT6{i}", [128, 128], BF16)) for i in range(4)]
        with nc.Block() as blk:
            L0_, S0 = n['ld'], n['s']

            @blk.sync
            def _(sy):
                for cc in range(NCH):
                    if cc >= 2:
                        sy.wait_ge(s_s, S0 + cc - 1)
                    sy.dma_start(wof[cc % 2][:], Wo_d[128 * cc:128 * (cc + 1), :]).then_inc(s_ld, 16)
                    n['ld'] += 1

            @blk.scalar
            def _(s):
                for cc in range(NCH):
                    s.wait_ge(s_ld, 16 * (L0_ + cc + 1))
                    s.activation(wobf[:, cc, :], wof[cc % 2][:], AF.Copy).then_inc(s_s, 1)
                    n['s'] += 1

        with nc.Block() as blk:
            T0, V0, S0, ST0 = n['t'], n['v'], n['s'], n['st']
            tr_inc = {}
            wo_done = {}

            @blk.tensor
            def _(t):
                for j in range(NT + 1):
                    if j < NT:
                        for cc in range(NCH):
                            k = j * NCH + cc
                            if k >= 4:
                                t.wait_ge(s_v, V0 + k - 3)
                            t.transpose(psT6[k % 4][:], vbuf[:, j, 128 * cc:128 * (cc + 1)],
                                        eye_s[:]).then_inc(s_t, 1)
                            n['t'] += 1
                            tr_inc[(j, cc)] = n['t']
                        if j == NT - 1:
                            for d in range(2):
                                k = NT * NCH + d
                                t.wait_ge(s_v, V0 + k - 3)
                                t.transpose(psT6[k % 4][:], eye_s[:], eye_s[:]).then_inc(s_t, 1)
                                n['t'] += 1
                                tr_inc[(NT, d)] = n['t']
                    if j >= 1:
                        jw = j - 1
                        pp = psS[jw % 2]
                        if jw >= 2:
                            t.wait_ge(s_s, S0 + jw - 1)
                        t.wait_ge(s_st, 16 * (ST0 + jw + 1))
                        for cc in range(NCH):
                            for i in range(2):
                                ins = t.matmul(pp[:, 512 * i:512 * (i + 1)],
                                               glhs[jw % 2][:, cc, :],
                                               wobf[:, cc, 512 * i:512 * (i + 1)],
                                               start=(cc == 0), stop=False)
                            ins.then_inc(s_t, 1)
                            n['t'] += 1
                        for i in range(2):
                            ins = t.matmul(pp[:, 512 * i:512 * (i + 1)], ones_r[:],
                                           bo_bf[:, 512 * i:512 * (i + 1)],
                                           start=False, stop=True)
                        ins.then_inc(s_t, 1)
                        n['t'] += 1
                        wo_done[jw] = n['t']

            @blk.vector
            def _(v):
                def tr2(j, cc):
                    k = j * NCH + cc + 2
                    return tr_inc[(k // NCH, k % NCH)] if k < NT * NCH else tr_inc[(NT, k - NT * NCH)]
                for j in range(NT):
                    for cc in range(NCH):
                        v.wait_ge(s_t, tr2(j, cc))
                        if j >= 2:
                            v.wait_ge(s_st, 16 * (ST0 + j - 1))
                        v.tensor_copy(gstg[j % 2][:, cc, :], psT6[(j * NCH + cc) % 4][:]).then_inc(s_v, 1)
                        n['v'] += 1

            @blk.gpsimd
            def _(g):
                for j in range(NT):
                    g.wait_ge(s_v, V0 + (j + 1) * NCH)
                    g.dma_start(glhs[j % 2].ap(), gstg[j % 2].ap()).then_inc(s_st, 16)
                    n['st'] += 1

            @blk.scalar
            def _(s):
                for j in range(NT):
                    s.wait_ge(s_t, wo_done[j])
                    s.activation(sbf[:, j, :], psS[j % 2][:], AF.Copy).then_inc(s_s, 1)
                    n['s'] += 1

    if debug:
        with nc.Block() as blk:
            @blk.gpsimd
            def _(g):
                g.dma_start(dbg["dbg_s"].ap(), sbf[:, :, :]).then_inc(s_st, 16)
                n['st'] += 1
                g.wait_ge(s_st, 16 * n['st'])

    if max_phase < 7:
        return finish()

    es_mid.close()
    with contextlib.ExitStack() as ps7:
        h0 = ps7.enter_context(nc.sbuf_tensor("h0", [128, NJ, B], BF16))
        h1 = ps7.enter_context(nc.sbuf_tensor("h1", [128, NJ, B], BF16))
        gT = ps7.enter_context(nc.sbuf_tensor("gT", [128, NCH, B], BF16))
        wsl = [ps7.enter_context(nc.sbuf_tensor(f"wsl{i}", [128, 4, 1024], BF16)) for i in range(4)]
        gch = ps7.enter_context(nc.sbuf_tensor("gch", [128, D], BF16))
        xch7 = [ps7.enter_context(nc.sbuf_tensor(f"xch7{i}", [128, D], F32)) for i in range(2)]
        och = [ps7.enter_context(nc.sbuf_tensor(f"och{i}", [128, D], F32)) for i in range(2)]
        sgs = ps7.enter_context(nc.sbuf_tensor("sgs", [128, D], F32))

        for tb in range(NB):
            tok = slice(tb * B, (tb + 1) * B)
            psb = contextlib.ExitStack()
            psL = [psb.enter_context(nc.psum_tensor(f"psL{tb}_{i}", [128, B], F32))
                   for i in range(8)]
            with nc.Block() as blk:
                L0_, T0, S0 = n['ld'], n['t'], n['s']
                kw_barrier = n['st']
                NRB0 = NCH // 4

                @blk.sync
                def _(sy):
                    sy.wait_ge(s_st, 16 * kw_barrier)
                    for g in range(4):
                        for rb in range(NRB0):
                            k = g * NRB0 + rb
                            if k >= 4:
                                sy.wait_ge(s_t, T0 + 4 * (k - 3))
                            sy.dma_start(wsl[k % 4].ap(),
                                         m0_d.ap().transpose([2, 0, 1, 3])[:, g, 4 * rb:4 * rb + 4, :]).then_inc(s_ld, 16)
                            n['ld'] += 1

                @blk.tensor
                def _(t):
                    t.wait_ge(s_ld, 16 * (L0_ + 1))
                    t.drain()
                    for g in range(4):
                        if g >= 1:
                            t.wait_ge(s_s, S0 + 8 * g)
                        for rb in range(NRB0):
                            k = g * NRB0 + rb
                            t.wait_ge(s_ld, 16 * (L0_ + min(k + 2, 4 * NRB0)))
                            for ri in range(4):
                                f = 4 * rb + ri
                                for jj in range(8):
                                    ins = t.matmul(psL[jj][:],
                                                   wsl[k % 4][:, ri, 128 * jj:128 * (jj + 1)],
                                                   xnT[:, f, tok], start=(f == 0),
                                                   stop=(f == NCH - 1))
                                ins.then_inc(s_t, 1)
                                n['t'] += 1

                @blk.scalar
                def _(s):
                    for g in range(4):
                        s.wait_ge(s_t, T0 + NCH * (g + 1))
                        for jj in range(4):
                            s.activation(h0[:, 8 * g + jj, :], psL[jj][:],
                                         AF.Relu, scale=sc2[:, 0:1]).then_inc(s_s, 1)
                            n['s'] += 1

                @blk.vector
                def _(v):
                    for g in range(4):
                        v.wait_ge(s_t, T0 + NCH * (g + 1))
                        for jj in range(4, 8):
                            v.tensor_scalar(h0[:, 8 * g + jj, :], psL[jj][:], sc2[:, 0:1],
                                            0.0, ALU.mult, ALU.max).then_inc(s_s, 1)
                            n['s'] += 1

            with nc.Block() as blk:
                L0_, T0, S0 = n['ld'], n['t'], n['s']
                NRB1 = NJ // 4

                @blk.sync
                def _(sy):
                    for g in range(4):
                        for rb in range(NRB1):
                            k = g * NRB1 + rb
                            if k >= 4:
                                sy.wait_ge(s_t, T0 + 4 * (k - 3))
                            sy.dma_start(wsl[k % 4].ap(),
                                         k1bf_d.ap().transpose([2, 0, 1, 3])[:, g, 4 * rb:4 * rb + 4, :]).then_inc(s_ld, 16)
                            n['ld'] += 1

                @blk.tensor
                def _(t):
                    t.wait_ge(s_ld, 16 * (L0_ + 1))
                    t.drain()
                    for g in range(4):
                        if g >= 1:
                            t.wait_ge(s_s, S0 + 8 * g)
                        for rb in range(NRB1):
                            k = g * NRB1 + rb
                            t.wait_ge(s_ld, 16 * (L0_ + min(k + 2, 4 * NRB1)))
                            for ri in range(4):
                                j1 = 4 * rb + ri
                                for jj in range(8):
                                    ins = t.matmul(psL[jj][:],
                                                   wsl[k % 4][:, ri, 128 * jj:128 * (jj + 1)],
                                                   h0[:, j1, :], start=(j1 == 0),
                                                   stop=(j1 == NJ - 1))
                                ins.then_inc(s_t, 1)
                                n['t'] += 1

                @blk.scalar
                def _(s):
                    for g in range(4):
                        s.wait_ge(s_t, T0 + NJ * (g + 1))
                        for jj in range(4):
                            s.activation(h1[:, 8 * g + jj, :], psL[jj][:],
                                         AF.Relu, scale=sc2[:, 1:2]).then_inc(s_s, 1)
                            n['s'] += 1

                @blk.vector
                def _(v):
                    for g in range(4):
                        v.wait_ge(s_t, T0 + NJ * (g + 1))
                        for jj in range(4, 8):
                            v.tensor_scalar(h1[:, 8 * g + jj, :], psL[jj][:], sc2[:, 1:2],
                                            0.0, ALU.mult, ALU.max).then_inc(s_s, 1)
                            n['s'] += 1

            with nc.Block() as blk:
                L0_, T0, S0 = n['ld'], n['t'], n['s']
                NRB2 = NJ // 4

                @blk.sync
                def _(sy):
                    for rb in range(NRB2):
                        k = rb
                        if k >= 4:
                            sy.wait_ge(s_t, T0 + 4 * (k - 3))
                        sy.dma_start(wsl[k % 4].ap(),
                                     k2bf_d.ap().transpose([2, 0, 1, 3])[:, 0, 4 * rb:4 * rb + 4, :]).then_inc(s_ld, 16)
                        n['ld'] += 1

                @blk.tensor
                def _(t):
                    t.wait_ge(s_ld, 16 * (L0_ + 1))
                    t.drain()
                    for rb in range(NRB2):
                        k = rb
                        t.wait_ge(s_ld, 16 * (L0_ + min(k + 2, NRB2)))
                        for ri in range(4):
                            j1 = 4 * rb + ri
                            for jj in range(8):
                                ins = t.matmul(psL[jj][:],
                                               wsl[k % 4][:, ri, 128 * jj:128 * (jj + 1)],
                                               h1[:, j1, :], start=(j1 == 0),
                                               stop=(j1 == NJ - 1))
                            ins.then_inc(s_t, 1)
                            n['t'] += 1

                @blk.scalar
                def _(s):
                    s.wait_ge(s_t, T0 + NJ)
                    for jj in range(8):
                        s.activation(gT[:, jj, :], psL[jj][:],
                                     AF.Silu, scale=sc2[:, 2:3]).then_inc(s_s, 1)
                        n['s'] += 1

            psb.close()
            psb2 = contextlib.ExitStack()
            psT7 = [psb2.enter_context(nc.psum_tensor(f"psT7{tb}_{i}", [128, 128], BF16))
                    for i in range(4)]
            with nc.Block() as blk:
                L0_, V0, T0, ST0 = n['ld'], n['v'], n['t'], n['st']

                @blk.sync
                def _(sy):
                    for ts_ in range(4):
                        j = tb * 4 + ts_
                        if ts_ >= 2:
                            sy.wait_ge(s_v, V0 + 9 * (ts_ - 1))
                        sy.dma_start(xch7[ts_ % 2][:], x_d[128 * j:128 * (j + 1), :]).then_inc(s_ld, 16)
                        n['ld'] += 1

                @blk.tensor
                def _(t):
                    k = 0
                    for ts_ in range(4):
                        for cc in range(NCH):
                            if k >= 4:
                                t.wait_ge(s_v, V0 + (k - 4) + (k - 4) // NCH + 1)
                            t.transpose(psT7[k % 4][:], gT[:, cc, 128 * ts_:128 * (ts_ + 1)],
                                        eye_s[:]).then_inc(s_t, 1)
                            n['t'] += 1
                            k += 1
                    for d in range(2):
                        t.wait_ge(s_v, V0 + (k - 4) + (k - 4) // NCH + 1)
                        t.transpose(psT7[k % 4][:], eye_s[:], eye_s[:]).then_inc(s_t, 1)
                        n['t'] += 1
                        k += 1

                @blk.vector
                def _(v):
                    k = 0
                    for ts_ in range(4):
                        j = tb * 4 + ts_
                        for cc in range(NCH):
                            v.wait_ge(s_t, T0 + k + 3)
                            v.tensor_copy(gch[:, 128 * cc:128 * (cc + 1)],
                                          psT7[k % 4][:]).then_inc(s_v, 1)
                            n['v'] += 1
                            k += 1
                        v.tensor_tensor(sgs[:], sbf[:, j, :], gch[:], ALU.mult)
                        v.wait_ge(s_ld, 16 * (L0_ + ts_ + 1))
                        if ts_ >= 2:
                            v.wait_ge(s_st, 16 * (ST0 + ts_ - 1))
                        v.tensor_add(och[ts_ % 2][:], xch7[ts_ % 2][:], sgs[:]).then_inc(s_v, 1)
                        n['v'] += 1

                @blk.gpsimd
                def _(g):
                    for ts_ in range(4):
                        j = tb * 4 + ts_
                        g.wait_ge(s_v, V0 + 9 * (ts_ + 1))
                        g.dma_start(out_d[128 * j:128 * (j + 1), :], och[ts_ % 2][:]).then_inc(s_st, 16)
                        n['st'] += 1
            psb2.close()

    return finish()


_NC_CACHE = {}


def _get_nc(debug=False):
    key = bool(debug)
    if key not in _NC_CACHE:
        _NC_CACHE[key] = build_nc(debug=key)
    return _NC_CACHE[key]


def _in_maps(inputs):
    hc = _host_consts()
    sc = np.empty((128, 3), np.float32)
    sc[:, 0] = np.float32(inputs['L0_ffn_s0'])
    sc[:, 1] = np.float32(inputs['L0_ffn_s1'])
    sc[:, 2] = np.float32(inputs['L0_ffn_s2'])
    shared = {
        "Wu": np.ascontiguousarray(inputs['L0_attn_Wu'], np.float32),
        "Wv": np.ascontiguousarray(inputs['L0_attn_Wv'], np.float32),
        "Wo": np.ascontiguousarray(inputs['L0_attn_Wo'], np.float32),
        "bu": np.ascontiguousarray(np.asarray(inputs['L0_attn_bu']).reshape(1, D), np.float32),
        "bv": np.ascontiguousarray(np.asarray(inputs['L0_attn_bv']).reshape(1, D), np.float32),
        "bo": np.ascontiguousarray(np.asarray(inputs['L0_attn_bo']).reshape(1, D), np.float32),
        "W1": np.ascontiguousarray(inputs['L0_attn_W1'], np.float32),
        "W2": np.ascontiguousarray(inputs['L0_attn_W2'], np.float32),
        "W3": np.ascontiguousarray(inputs['L0_attn_W3'], np.float32),
        "b1": np.ascontiguousarray(np.asarray(inputs['L0_attn_b1']).reshape(64, 1), np.float32),
        "b2": np.ascontiguousarray(np.asarray(inputs['L0_attn_b2']).reshape(64, 1), np.float32),
        "b3": np.ascontiguousarray(np.asarray(inputs['L0_attn_b3']).reshape(1, D), np.float32),
        "k0a_b": np.ascontiguousarray(
            np.asarray(inputs['L0_ffn_k0']).view(np.uint8)[:D].reshape(NCH, 128, 4, 1024).transpose(2, 0, 1, 3)),
        "k0b_b": np.ascontiguousarray(
            np.asarray(inputs['L0_ffn_k0']).view(np.uint8)[D:].reshape(NCH, 128, 4, 1024).transpose(2, 0, 1, 3)),
        "k1_b": np.ascontiguousarray(
            np.asarray(inputs['L0_ffn_k1']).view(np.uint8).reshape(NJ, 128, 4, 1024).transpose(2, 0, 1, 3)),
        "k2_b": np.ascontiguousarray(
            np.asarray(inputs['L0_ffn_k2']).view(np.uint8).reshape(NJ, 128, 1, 1024).transpose(2, 0, 1, 3)),
        "sc": sc,
        "peT": hc['peT'],
        "dec": hc['dec'],
        "invcoef": hc['invcoef'],
        "eye": hc['eye'],
    }
    xs = np.ascontiguousarray(np.asarray(inputs['L0_attn_x']), np.float32)
    return [dict(shared, x=np.ascontiguousarray(xs[b])) for b in range(8)]


def kernel(**inputs):
    in_maps = _in_maps(inputs)
    nc = _get_nc(debug=False)
    res = run_bass_kernel_spmd(nc, in_maps, core_ids=list(range(8)))
    return np.stack([np.asarray(res.results[b]["out"]) for b in range(8)], axis=0)


# version 07
# speedup vs baseline: 1.0197x; 1.0197x over previous
import contextlib
import math

import numpy as np
import ml_dtypes

import concourse.bass as bass
import concourse.mybir as mybir
from concourse.bass_utils import run_bass_kernel_spmd

F32 = mybir.dt.float32
BF16 = mybir.dt.bfloat16
U8 = mybir.dt.uint8
I32 = mybir.dt.int32
AF = mybir.ActivationFunctionType
ALU = mybir.AluOpType
AX = mybir.AxisListType

BF = ml_dtypes.bfloat16

T = 2048
D = 1024
NT = T // 128
NCH = D // 128
NQ = 33
KF = NQ * 128
HKF = KF // 2
FD = 4096
NJ = FD // 128
NB = 4
B = T // NB
SIN_SCALE = -2.0 * math.pi / 4096.0

_CACHE = {}


def _host_consts():
    if _CACHE:
        return _CACHE
    import jax
    import jax.numpy as jnp
    cpu = jax.devices('cpu')[0]
    with jax.default_device(cpu):
        pos = jnp.arange(T, dtype=jnp.float32)[:, None]
        c64 = -(math.log(10000.0) / 64.0)
        div = jnp.exp(jnp.arange(0, 64, 2, dtype=jnp.float32) * np.float32(c64))
        prod = pos * div
        pe = np.zeros((T, 64), np.float32)
        pe[:, 0::2] = np.asarray(jnp.sin(prod))
        pe[:, 1::2] = np.asarray(jnp.cos(prod))
        tt = jnp.arange(T, dtype=jnp.float32)
        dec = np.asarray(jnp.exp(np.float32(-0.01) * tt))
    _CACHE['peT'] = np.ascontiguousarray(pe.T)
    _CACHE['dec'] = np.ascontiguousarray(dec.reshape(NT, 128).T)
    coef = np.full((128, NQ), 2.0 / 4096.0, np.float32)
    coef[0, 0] = 1.0 / 4096.0
    coef[0, 32] = 1.0 / 4096.0
    coef[1:, 32] = 0.0
    _CACHE['invcoef'] = coef
    _CACHE['eye'] = np.eye(128, dtype=np.float32).astype(BF)
    return _CACHE


def build_nc(debug=False, max_phase=99):
    nc = bass.Bass("TRN2", target_bir_lowering=False, detect_race_conditions=False)

    x_d = nc.dram_tensor("x", [T, D], F32, kind="ExternalInput")
    Wu_d = nc.dram_tensor("Wu", [D, D], F32, kind="ExternalInput")
    Wv_d = nc.dram_tensor("Wv", [D, D], F32, kind="ExternalInput")
    Wo_d = nc.dram_tensor("Wo", [D, D], F32, kind="ExternalInput")
    bu_d = nc.dram_tensor("bu", [1, D], F32, kind="ExternalInput")
    bv_d = nc.dram_tensor("bv", [1, D], F32, kind="ExternalInput")
    bo_d = nc.dram_tensor("bo", [1, D], F32, kind="ExternalInput")
    W1_d = nc.dram_tensor("W1", [64, 64], F32, kind="ExternalInput")
    W2_d = nc.dram_tensor("W2", [64, 64], F32, kind="ExternalInput")
    W3_d = nc.dram_tensor("W3", [64, D], F32, kind="ExternalInput")
    b1_d = nc.dram_tensor("b1", [64, 1], F32, kind="ExternalInput")
    b2_d = nc.dram_tensor("b2", [64, 1], F32, kind="ExternalInput")
    b3_d = nc.dram_tensor("b3", [1, D], F32, kind="ExternalInput")
    k0a_d = nc.dram_tensor("k0a_b", [4, NCH, 128, 1024], U8, kind="ExternalInput")
    k0b_d = nc.dram_tensor("k0b_b", [4, NCH, 128, 1024], U8, kind="ExternalInput")
    k1_d = nc.dram_tensor("k1_b", [4, NJ, 128, 1024], U8, kind="ExternalInput")
    k2_d = nc.dram_tensor("k2_b", [1, NJ, 128, 1024], U8, kind="ExternalInput")
    sc_d = nc.dram_tensor("sc", [128, 3], F32, kind="ExternalInput")
    peT_d = nc.dram_tensor("peT", [64, T], F32, kind="ExternalInput")
    dec_d = nc.dram_tensor("dec", [128, NT], F32, kind="ExternalInput")
    invc_d = nc.dram_tensor("invcoef", [128, NQ], F32, kind="ExternalInput")
    eye_d = nc.dram_tensor("eye", [128, 128], BF16, kind="ExternalInput")

    out_d = nc.dram_tensor("out", [T, D], F32, kind="ExternalOutput")
    fwdt_d = nc.dram_tensor("fwdt", [NT, 128, KF], BF16, kind="Internal")
    invt_d = nc.dram_tensor("invt", [NQ, 128, T], BF16, kind="Internal")
    m0_d = nc.dram_tensor("m0bf", [4, NCH, 128, 1024], BF16, kind="Internal")
    k1bf_d = nc.dram_tensor("k1bf", [4, NJ, 128, 1024], BF16, kind="Internal")
    k2bf_d = nc.dram_tensor("k2bf", [1, NJ, 128, 1024], BF16, kind="Internal")

    dbg = {}
    if debug:
        for nm, shp, dt in [("dbg_xnT", [128, NCH, T], BF16),
                            ("dbg_u", [128, NT, D], BF16),
                            ("dbg_v", [128, NT, D], BF16),
                            ("dbg_G", [65, KF], BF16),
                            ("dbg_Y", [128, NQ, D], BF16),
                            ("dbg_gated", [128, NT, D], BF16),
                            ("dbg_s", [128, NT, D], BF16),
                            ("dbg_Faug", [128, NT, 65], BF16),
                            ("dbg_h0", [128, NJ, B], BF16),
                            ("dbg_h1", [128, NJ, B], BF16),
                            ("dbg_gT", [128, NCH, B], BF16)]:
            dbg[nm] = nc.dram_tensor(nm, shp, dt, kind="ExternalOutput")
        dbg["dbg_m0"] = nc.dram_tensor("dbg_m0", [4, NCH, 128, 1024], BF16, kind="ExternalOutput")
        dbg["dbg_k1b"] = nc.dram_tensor("dbg_k1b", [4, NJ, 128, 1024], BF16, kind="ExternalOutput")
        dbg["dbg_k2b"] = nc.dram_tensor("dbg_k2b", [1, NJ, 128, 1024], BF16, kind="ExternalOutput")

    es = contextlib.ExitStack()

    def sb(name, shape, dtype):
        return es.enter_context(nc.sbuf_tensor(name, shape, dtype))

    xnT = sb("xnT", [128, NCH, T], BF16)
    sbf = sb("sbf", [128, NT, D], BF16)
    dec_s = sb("dec_s", [128, NT], F32)
    invc_s = sb("invc_s", [128, NQ], F32)
    eye_s = sb("eye_s", [128, 128], BF16)
    sc2 = sb("sc2", [128, 3], F32)
    bu_bf = sb("bu_bf", [1, D], BF16)
    bv_bf = sb("bv_bf", [1, D], BF16)
    bo_bf = sb("bo_bf", [1, D], BF16)
    ones_r = sb("ones_r", [1, 128], BF16)
    W3aug = sb("W3aug", [65, D], BF16)
    Faug = sb("Faug", [128, NT, 65], BF16)
    kvals = sb("kvals", [128, NQ], F32)
    tpall = sb("tpall", [128, NT], F32)
    lnA = sb("lnA", [128, NT], F32)
    lnB = sb("lnB", [128, NT], F32)
    es_mid = contextlib.ExitStack()
    G1 = es_mid.enter_context(nc.sbuf_tensor("G1", [65, KF], BF16))
    vbuf = es_mid.enter_context(nc.sbuf_tensor("vbuf", [128, NT, D], BF16))

    s_ld = es.enter_context(nc.semaphore("s_ld"))
    s_st = es.enter_context(nc.semaphore("s_st"))
    s_t = es.enter_context(nc.semaphore("s_t"))
    s_v = es.enter_context(nc.semaphore("s_v"))
    s_s = es.enter_context(nc.semaphore("s_s"))
    s_w = es.enter_context(nc.semaphore("s_w"))
    s_k = es.enter_context(nc.semaphore("s_k"))
    s_k2 = es.enter_context(nc.semaphore("s_k2"))
    s_kc = es.enter_context(nc.semaphore("s_kc"))

    n = dict(ld=0, st=0, t=0, v=0, s=0, w=0, k=0, k2=0, kc=0)

    def finish():
        with nc.Block() as blk:
            @blk.gpsimd
            def _(g):
                g.wait_ge(s_st, 16 * n['st'])
                if n['k2']:
                    g.wait_ge(s_k2, 16 * n['k2'])

            @blk.sync
            def _(sy):
                sy.wait_ge(s_ld, 16 * n['ld'])
                if n['k']:
                    sy.wait_ge(s_k, 16 * n['k'])
        es_mid.close()
        es.close()
        return nc

    ps = contextlib.ExitStack()
    peT_f = ps.enter_context(nc.sbuf_tensor("peT_f", [64, T], F32))
    peT_bf = ps.enter_context(nc.sbuf_tensor("peT_bf", [64, T], BF16))
    W1f = ps.enter_context(nc.sbuf_tensor("W1f", [64, 64], F32))
    W2f = ps.enter_context(nc.sbuf_tensor("W2f", [64, 64], F32))
    W3af = ps.enter_context(nc.sbuf_tensor("W3af", [65, D], F32))
    bxf = ps.enter_context(nc.sbuf_tensor("bxf", [1, 3 * D], F32))
    W1bf = ps.enter_context(nc.sbuf_tensor("W1bf", [64, 64], BF16))
    W2bf = ps.enter_context(nc.sbuf_tensor("W2bf", [64, 64], BF16))
    b1s = ps.enter_context(nc.sbuf_tensor("b1s", [64, 1], F32))
    b2s = ps.enter_context(nc.sbuf_tensor("b2s", [64, 1], F32))
    scf = ps.enter_context(nc.sbuf_tensor("scf", [128, 3], F32))
    F1g = ps.enter_context(nc.sbuf_tensor("F1g", [64, T], BF16))
    F2g = ps.enter_context(nc.sbuf_tensor("F2g", [128, T], BF16))
    xch = [ps.enter_context(nc.sbuf_tensor(f"xch{i}", [128, D], F32)) for i in range(2)]
    xnch = [ps.enter_context(nc.sbuf_tensor(f"xnch{i}", [128, D], BF16)) for i in range(2)]
    sqs = ps.enter_context(nc.sbuf_tensor("sqs", [128, D], F32))
    sums = ps.enter_context(nc.sbuf_tensor("sums", [128, NT], F32))
    sumsq = ps.enter_context(nc.sbuf_tensor("sumsq", [128, NT], F32))
    lw = [ps.enter_context(nc.sbuf_tensor(f"lw{i}", [128, NT], F32)) for i in range(3)]
    psF = ps.enter_context(nc.psum_tensor("psF", [64, T], F32))
    psTr = [ps.enter_context(nc.psum_tensor(f"psTr{i}", [128, 128], BF16)) for i in range(4)]

    with nc.Block() as blk:
        @blk.sync
        def _(sy):
            for ap_s, ap_d in [
                (peT_f[:], peT_d[:, :]), (W1f[:], W1_d[:, :]), (W2f[:], W2_d[:, :]),
                (W3af[0:64, :], W3_d[:, :]), (W3af[64:65, :], b3_d[:, :]),
                (b1s[:], b1_d[:, :]), (b2s[:], b2_d[:, :]),
                (bxf[:, 0:D], bu_d[:, :]), (bxf[:, D:2 * D], bv_d[:, :]),
                (bxf[:, 2 * D:3 * D], bo_d[:, :]),
                (dec_s[:], dec_d[:, :]), (invc_s[:], invc_d[:, :]),
                (eye_s[:], eye_d[:, :]), (scf[:], sc_d[:, :]),
            ]:
                sy.dma_start(ap_s, ap_d).then_inc(s_ld, 16)
                n['ld'] += 1

        @blk.scalar
        def _(s):
            s.wait_ge(s_ld, 16 * n['ld'])
            s.activation(peT_bf[:], peT_f[:], AF.Copy)
            s.activation(W1bf[:], W1f[:], AF.Copy)
            s.activation(W2bf[:], W2f[:], AF.Copy)
            s.activation(W3aug[:], W3af[:], AF.Copy)
            s.activation(bu_bf[:], bxf[:, 0:D], AF.Copy)
            s.activation(bv_bf[:], bxf[:, D:2 * D], AF.Copy)
            s.activation(bo_bf[:], bxf[:, 2 * D:3 * D], AF.Copy)

        @blk.vector
        def _(v):
            v.wait_ge(s_ld, 16 * n['ld'])
            v.memset(ones_r[:], 1.0)
            v.tensor_scalar_mul(sc2[:], scf[:], 2.0)
            for j in range(NT):
                v.memset(Faug[:, j, 64:65], 1.0)
            v.memset(F2g[64:128, :], 0.0)

        @blk.gpsimd
        def _(g):
            g.iota(tpall[:], [[128, NT]], base=0, channel_multiplier=1,
                   allow_small_or_imprecise_dtypes=True)
            g.iota(kvals[:, 0:32], [[128, 16], [0, 2]], base=0, channel_multiplier=1,
                   allow_small_or_imprecise_dtypes=True)
            g.iota(kvals[:, 32:33], [[0, 1]], base=2048, channel_multiplier=1,
                   allow_small_or_imprecise_dtypes=True)

    if max_phase < 0.2:
        ps.close()
        return finish()

    with nc.Block() as blk:
        @blk.tensor
        def _(t):
            for i in range(T // 512):
                ins = t.matmul(psF[:, 512 * i:512 * (i + 1)], W1bf[:],
                               peT_bf[:, 512 * i:512 * (i + 1)], start=True, stop=True)
            ins.then_inc(s_t, 1)
            n['t'] += 1

        @blk.scalar
        def _(s):
            s.wait_ge(s_t, n['t'])
            s.activation(F1g[:], psF[:], AF.Gelu_apprx_tanh, bias=b1s[:])

    with nc.Block() as blk:
        @blk.tensor
        def _(t):
            for i in range(T // 512):
                ins = t.matmul(psF[:, 512 * i:512 * (i + 1)], W2bf[:],
                               F1g[:, 512 * i:512 * (i + 1)], start=True, stop=True)
            ins.then_inc(s_t, 1)
            n['t'] += 1

        @blk.scalar
        def _(s):
            s.wait_ge(s_t, n['t'])
            s.activation(F2g[0:64, :], psF[:], AF.Gelu_apprx_tanh, bias=b2s[:])

    if max_phase < 0.4:
        ps.close()
        return finish()

    with nc.Block() as blk:
        T0, V0 = n['t'], n['v']

        @blk.tensor
        def _(t):
            for j in range(NT + 2):
                if j >= 4:
                    t.wait_ge(s_v, V0 + j - 3)
                src_ap = F2g[:, 128 * j:128 * (j + 1)] if j < NT else eye_s[:]
                t.transpose(psTr[j % 4][:], src_ap, eye_s[:]).then_inc(s_t, 1)
                n['t'] += 1

        @blk.vector
        def _(v):
            for j in range(NT):
                v.wait_ge(s_t, T0 + j + 3)
                v.tensor_copy(Faug[:, j, 0:64], psTr[j % 4][:, 0:64]).then_inc(s_v, 1)
                n['v'] += 1

    if max_phase < 0.45:
        ps.close()
        return finish()

    with nc.Block() as blk:
        L0_, S0 = n['ld'], n['s']

        @blk.sync
        def _(sy):
            for j in range(NT):
                if j >= 2:
                    sy.wait_ge(s_s, S0 + 2 * (j - 1))
                sy.dma_start(xch[j % 2][:], x_d[128 * j:128 * (j + 1), :]).then_inc(s_ld, 16)
                n['ld'] += 1

        @blk.scalar
        def _(s):
            for j in range(NT):
                s.wait_ge(s_ld, 16 * (L0_ + j + 1))
                s.activation(sqs[:], xch[j % 2][:], AF.Identity,
                             accum_out=sums[:, j:j + 1]).then_inc(s_s, 1)
                s.activation(sqs[:], xch[j % 2][:], AF.Square,
                             accum_out=sumsq[:, j:j + 1]).then_inc(s_s, 1)
                n['s'] += 2

    with nc.Block() as blk:
        V0, S0 = n['v'], n['s']

        @blk.vector
        def _(v):
            v.tensor_scalar_mul(lw[0][:], sums[:], -1.0 / D)
            v.tensor_scalar(lw[1][:], sumsq[:], 1.0 / D, 1e-5,
                            ALU.mult, ALU.add).then_inc(s_v, 1)
            n['v'] += 1
            v.wait_ge(s_v, V0 + 1)
            v.tensor_tensor(lw[2][:], lw[0][:], lw[0][:], ALU.mult).then_inc(s_v, 1)
            n['v'] += 1
            v.wait_ge(s_v, V0 + 2)
            v.tensor_sub(lw[1][:], lw[1][:], lw[2][:]).then_inc(s_v, 1)
            n['v'] += 1

        @blk.scalar
        def _(s):
            s.wait_ge(s_v, V0 + 3)
            s.activation(lw[2][:], lw[1][:], AF.Sqrt).then_inc(s_s, 1)
            n['s'] += 1

        @blk.vector
        def _(v):
            v.wait_ge(s_s, S0 + 1)
            v.reciprocal(lnA[:], lw[2][:]).then_inc(s_v, 1)
            n['v'] += 1
            v.wait_ge(s_v, V0 + 4)
            v.tensor_tensor(lnB[:], lw[0][:], lnA[:], ALU.mult).then_inc(s_v, 1)
            n['v'] += 1

    if max_phase < 0.48:
        ps.close()
        return finish()

    with nc.Block() as blk:
        L0_, S0, T0, W0 = n['ld'], n['s'], n['t'], n['w']

        @blk.sync
        def _(sy):
            for j in range(NT):
                if j >= 2:
                    sy.wait_ge(s_s, S0 + j - 1)
                sy.dma_start(xch[j % 2][:], x_d[128 * j:128 * (j + 1), :]).then_inc(s_ld, 16)
                n['ld'] += 1

        @blk.scalar
        def _(s):
            for j in range(NT):
                s.wait_ge(s_ld, 16 * (L0_ + j + 1))
                if j >= 1:
                    s.wait_ge(s_t, T0 + NCH * (j - 1))
                s.activation(xnch[j % 2][:], xch[j % 2][:], AF.Identity,
                             scale=lnA[:, j:j + 1], bias=lnB[:, j:j + 1]).then_inc(s_s, 1)
                n['s'] += 1
            s.activation(sqs[:], xch[(NT - 1) % 2][:], AF.Square).then_inc(s_s, 1)
            n['s'] += 1

        @blk.tensor
        def _(t):
            k = 0
            for j in range(NT):
                t.wait_ge(s_s, S0 + min(j + 2, NT + 1))
                for cc in range(NCH):
                    if k >= 4:
                        t.wait_ge(s_w, W0 + k - 3)
                    t.transpose(psTr[k % 4][:], xnch[j % 2][:, 128 * cc:128 * (cc + 1)],
                                eye_s[:]).then_inc(s_t, 1)
                    n['t'] += 1
                    k += 1
            for d in range(2):
                t.wait_ge(s_w, W0 + k - 3)
                t.transpose(psTr[k % 4][:], eye_s[:], eye_s[:]).then_inc(s_t, 1)
                n['t'] += 1
                k += 1

        @blk.vector
        def _(v):
            k = 0
            for j in range(NT):
                for cc in range(NCH):
                    v.wait_ge(s_t, T0 + k + 3)
                    v.tensor_copy(xnT[:, cc, 128 * j:128 * (j + 1)],
                                  psTr[k % 4][:]).then_inc(s_w, 1)
                    n['w'] += 1
                    k += 1

    ps.close()
    if max_phase < 1:
        return finish()

    with contextlib.ExitStack() as ps1:
        krow = ps1.enter_context(nc.sbuf_tensor("krow", [128, KF], F32))
        trow = ps1.enter_context(nc.sbuf_tensor("trow", [128, T], F32))
        ktf = ps1.enter_context(nc.sbuf_tensor("ktf", [128, KF], F32))
        kti = ps1.enter_context(nc.sbuf_tensor("kti", [128, KF], I32))
        ktm = ps1.enter_context(nc.sbuf_tensor("ktm", [128, KF], F32))
        stb = [ps1.enter_context(nc.sbuf_tensor(f"stb{i}", [128, KF], BF16)) for i in range(2)]

        with nc.Block() as blk:
            @blk.gpsimd
            def _(g):
                g.iota(krow[:, 0:4096], [[128, 16], [0, 2], [1, 128]], base=0,
                       channel_multiplier=0, allow_small_or_imprecise_dtypes=True)
                g.iota(krow[:, 4096:KF], [[1, 128]], base=2048,
                       channel_multiplier=0, allow_small_or_imprecise_dtypes=True)
                g.iota(trow[:], [[1, T]], base=0, channel_multiplier=0,
                       allow_small_or_imprecise_dtypes=True)

        with nc.Block() as blk:
            V0, S0, ST0 = n['v'], n['s'], n['st']
            kre = ktf[:, 0:4096].rearrange("p (m two f) -> p m two f", two=2, f=128)
            krw = krow[:, 0:4096].rearrange("p (m two f) -> p m two f", two=2, f=128)

            @blk.vector
            def _(v):
                for j in range(NT):
                    v.tensor_scalar(kre[:, :, 0, :], krw[:, :, 0, :], tpall[:, j:j + 1],
                                    1024.0, ALU.mult, ALU.add)
                    v.tensor_scalar(kre[:, :, 1, :], krw[:, :, 1, :], tpall[:, j:j + 1],
                                    2048.0, ALU.mult, ALU.add)
                    v.tensor_scalar(ktf[:, 4096:KF], krow[:, 4096:KF], tpall[:, j:j + 1],
                                    1024.0, ALU.mult, ALU.add)
                    v.tensor_copy(kti[:], ktf[:])
                    v.tensor_single_scalar(kti[:], kti[:], 4095, ALU.bitwise_and)
                    if j >= 1:
                        v.wait_ge(s_s, S0 + j)
                    v.tensor_scalar(ktm[:], kti[:], -2048.0, None, ALU.add).then_inc(s_v, 1)
                    n['v'] += 1

            @blk.scalar
            def _(s):
                for j in range(NT):
                    s.wait_ge(s_v, V0 + j + 1)
                    if j >= 2:
                        s.wait_ge(s_st, 16 * (ST0 + j - 1))
                    s.activation(stb[j % 2][:], ktm[:], AF.Sin,
                                 scale=SIN_SCALE).then_inc(s_s, 1)
                    n['s'] += 1

            @blk.gpsimd
            def _(g):
                for j in range(NT):
                    g.wait_ge(s_s, S0 + j + 1)
                    g.dma_start(fwdt_d[j, :, :], stb[j % 2][:]).then_inc(s_st, 16)
                    n['st'] += 1

        with nc.Block() as blk:
            V0, S0, ST0 = n['v'], n['s'], n['st']

            @blk.vector
            def _(v):
                for q in range(NQ):
                    cq = 1024.0 if (q % 2 == 0 or q == 32) else 2048.0
                    v.tensor_scalar(ktf[:, 0:T], trow[:], kvals[:, q:q + 1],
                                    cq, ALU.mult, ALU.add)
                    v.tensor_copy(kti[:, 0:T], ktf[:, 0:T])
                    v.tensor_single_scalar(kti[:, 0:T], kti[:, 0:T], 4095, ALU.bitwise_and)
                    if q >= 1:
                        v.wait_ge(s_s, S0 + q)
                    v.tensor_scalar(ktm[:, 0:T], kti[:, 0:T], -2048.0, None,
                                    ALU.add).then_inc(s_v, 1)
                    n['v'] += 1
                    if q >= 1:
                        if q >= 3:
                            v.wait_ge(s_st, 16 * (ST0 + q - 2))
                        v.tensor_scalar(stb[(q - 1) % 2][:, 0:T], ktf[:, KF - T:KF],
                                        invc_s[:, q - 1:q], None, ALU.mult).then_inc(s_v, 1)
                        n['v'] += 1
                v.wait_ge(s_s, S0 + NQ)
                v.wait_ge(s_st, 16 * (ST0 + NQ - 2))
                v.tensor_scalar(stb[(NQ - 1) % 2][:, 0:T], ktf[:, KF - T:KF],
                                invc_s[:, NQ - 1:NQ], None, ALU.mult).then_inc(s_v, 1)
                n['v'] += 1

            @blk.scalar
            def _(s):
                for q in range(NQ):
                    s.wait_ge(s_v, V0 + (2 * q + 1 if q >= 1 else 1))
                    s.activation(ktf[:, KF - T:KF], ktm[:, 0:T], AF.Sin,
                                 scale=SIN_SCALE).then_inc(s_s, 1)
                    n['s'] += 1

            @blk.gpsimd
            def _(g):
                for q in range(NQ):
                    coef_inc = 2 * q + 3 if q < NQ - 1 else 2 * NQ
                    g.wait_ge(s_v, V0 + coef_inc)
                    g.dma_start(invt_d[q, :, :], stb[q % 2][:, 0:T]).then_inc(s_st, 16)
                    n['st'] += 1

    if max_phase < 2:
        return finish()

    with contextlib.ExitStack() as ps2:
        dbuf = [ps2.enter_context(nc.sbuf_tensor(f"dbuf{i}", [128, HKF], BF16)) for i in range(2)]
        ddec = [ps2.enter_context(nc.sbuf_tensor(f"ddec{i}", [128, HKF], BF16)) for i in range(2)]
        psG = ps2.enter_context(nc.psum_tensor("psG", [65, HKF], F32))
        with nc.Block() as blk:
            L0_, V0, T0, S0 = n['ld'], n['v'], n['t'], n['s']

            @blk.sync
            def _(sy):
                sy.wait_ge(s_st, 16 * n['st'])
                for h in range(2):
                    for dc in range(NT):
                        k = h * NT + dc
                        if k >= 2:
                            sy.wait_ge(s_v, V0 + k - 1)
                        sy.dma_start(dbuf[k % 2][:],
                                     fwdt_d[dc, :, h * HKF:(h + 1) * HKF]).then_inc(s_ld, 16)
                        n['ld'] += 1

            @blk.vector
            def _(v):
                for h in range(2):
                    for dc in range(NT):
                        k = h * NT + dc
                        v.wait_ge(s_ld, 16 * (L0_ + k + 1))
                        if k >= 2:
                            v.wait_ge(s_t, T0 + k - 1)
                        v.tensor_scalar(ddec[k % 2][:], dbuf[k % 2][:],
                                        dec_s[:, dc:dc + 1], None, ALU.mult).then_inc(s_v, 1)
                        n['v'] += 1

            @blk.tensor
            def _(t):
                for h in range(2):
                    if h >= 1:
                        t.wait_ge(s_s, S0 + h)
                    for dc in range(NT):
                        k = h * NT + dc
                        t.wait_ge(s_v, V0 + k + 1)
                        for i in range(5):
                            c0, c1 = 512 * i, min(512 * (i + 1), HKF)
                            ins = t.matmul(psG[:, c0:c1], Faug[:, dc, :], ddec[k % 2][:, c0:c1],
                                           start=(dc == 0), stop=(dc == NT - 1))
                        ins.then_inc(s_t, 1)
                        n['t'] += 1

            @blk.scalar
            def _(s):
                for h in range(2):
                    s.wait_ge(s_t, T0 + (h + 1) * NT)
                    s.activation(G1[:, h * HKF:(h + 1) * HKF], psG[:], AF.Copy).then_inc(s_s, 1)
                    n['s'] += 1

    if max_phase < 2.5:
        return finish()

    with contextlib.ExitStack() as psk:
        kua = psk.enter_context(nc.sbuf_tensor("kua", [128, 2, 4, 1024], U8))
        kub = psk.enter_context(nc.sbuf_tensor("kub", [128, 2, 4, 1024], U8))
        kwo = psk.enter_context(nc.sbuf_tensor("kwo", [128, 2, 4, 1024], BF16))
        with nc.Block() as blk:
            L0_, V0, S0, ST0 = n['ld'], n['v'], n['s'], n['st']
            jobs = [(k0a_d, k0b_d, m0_d, g, 4 * rb) for g in range(4) for rb in range(NCH // 4)]

            @blk.sync
            def _(sy):
                for i, (sa, sb, dst, g, r0) in enumerate(jobs):
                    if i >= 2:
                        sy.wait_ge(s_v, V0 + i - 1)
                    sy.dma_start(kua[:, i % 2, :, :],
                                 sa.ap().transpose([2, 0, 1, 3])[:, g, r0:r0 + 4, :]).then_inc(s_ld, 16)
                    n['ld'] += 1
                    if sb is not None:
                        sy.dma_start(kub[:, i % 2, :, :],
                                     sb.ap().transpose([2, 0, 1, 3])[:, g, r0:r0 + 4, :]).then_inc(s_ld, 16)
                        n['ld'] += 1

            ld_at = []
            acc = L0_
            for (sa, sb, dst, g, r0) in jobs:
                acc += 2 if sb is not None else 1
                ld_at.append(acc)

            @blk.vector
            def _(v):
                for i, (sa, sb, dst, g, r0) in enumerate(jobs):
                    v.wait_ge(s_ld, 16 * ld_at[i])
                    if i >= 2:
                        v.wait_ge(s_st, 16 * (ST0 + i - 1))
                    if sb is not None:
                        v.scalar_tensor_tensor(kwo[:, i % 2, :, :], kua[:, i % 2, :, :], 1.0,
                                               kub[:, i % 2, :, :], ALU.mult,
                                               ALU.subtract).then_inc(s_v, 1)
                    else:
                        v.tensor_scalar(kwo[:, i % 2, :, :], kua[:, i % 2, :, :], 2.0, -1.0,
                                        ALU.mult, ALU.add).then_inc(s_v, 1)
                    n['v'] += 1

            @blk.gpsimd
            def _(g_):
                for i, (sa, sb, dst, g, r0) in enumerate(jobs):
                    g_.wait_ge(s_v, V0 + i + 1)
                    g_.dma_start(dst.ap().transpose([2, 0, 1, 3])[:, g, r0:r0 + 4, :],
                                 kwo[:, i % 2, :, :]).then_inc(s_st, 16)
                    n['st'] += 1

    if max_phase < 3:
        return finish()

    for (W_src, b_bf, obuf, tag) in [(Wu_d, bu_bf, sbf, "u"), (Wv_d, bv_bf, vbuf, "v")]:
        with contextlib.ExitStack() as ps3:
            wf = [ps3.enter_context(nc.sbuf_tensor(f"wf{tag}{i}", [128, D], F32)) for i in range(2)]
            wbf = ps3.enter_context(nc.sbuf_tensor(f"wbf{tag}", [128, NCH, D], BF16))
            psU = [ps3.enter_context(nc.psum_tensor(f"psU{tag}{i}", [128, D], F32)) for i in range(2)]
            with nc.Block() as blk:
                L0_, S0 = n['ld'], n['s']

                @blk.sync
                def _(sy):
                    for cc in range(NCH):
                        if cc >= 2:
                            sy.wait_ge(s_s, S0 + cc - 1)
                        sy.dma_start(wf[cc % 2][:], W_src[128 * cc:128 * (cc + 1), :]).then_inc(s_ld, 16)
                        n['ld'] += 1

                @blk.scalar
                def _(s):
                    for cc in range(NCH):
                        s.wait_ge(s_ld, 16 * (L0_ + cc + 1))
                        s.activation(wbf[:, cc, :], wf[cc % 2][:], AF.Copy).then_inc(s_s, 1)
                        n['s'] += 1

            with nc.Block() as blk:
                T0, S0 = n['t'], n['s']

                @blk.tensor
                def _(t):
                    for j in range(NT):
                        if j >= 2:
                            t.wait_ge(s_s, S0 + j - 1)
                        pp = psU[j % 2]
                        for cc in range(NCH):
                            for i in range(2):
                                t.matmul(pp[:, 512 * i:512 * (i + 1)],
                                         xnT[:, cc, 128 * j:128 * (j + 1)],
                                         wbf[:, cc, 512 * i:512 * (i + 1)],
                                         start=(cc == 0), stop=False)
                        for i in range(2):
                            ins = t.matmul(pp[:, 512 * i:512 * (i + 1)], ones_r[:],
                                           b_bf[:, 512 * i:512 * (i + 1)],
                                           start=False, stop=True)
                        ins.then_inc(s_t, 1)
                        n['t'] += 1

                @blk.scalar
                def _(s):
                    for j in range(NT):
                        s.wait_ge(s_t, T0 + j + 1)
                        s.activation(obuf[:, j, :], psU[j % 2][:], AF.Copy).then_inc(s_s, 1)
                        n['s'] += 1

    if max_phase < 4:
        return finish()

    with contextlib.ExitStack() as psY:
        Yspec = psY.enter_context(nc.sbuf_tensor("Yspec", [128, NQ, D], BF16))

        with contextlib.ExitStack() as ps4:
            ftile = [ps4.enter_context(nc.sbuf_tensor(f"ftile{i}", [128, NT, 128], BF16)) for i in range(3)]
            Hsb = [ps4.enter_context(nc.sbuf_tensor(f"Hsb{i}", [128, D], BF16)) for i in range(2)]
            tp1 = ps4.enter_context(nc.sbuf_tensor("tp1", [128, D], F32))
            tp2 = ps4.enter_context(nc.sbuf_tensor("tp2", [128, D], F32))
            psV = [ps4.enter_context(nc.psum_tensor(f"psV{i}", [128, D], F32)) for i in range(3)]
            psH = ps4.enter_context(nc.psum_tensor("psH", [128, D], F32))
            fsrc = fwdt_d.ap().transpose([1, 0, 2])

            with nc.Block() as blk:
                L0_, T0, S0, V0, ST4_ = n['ld'], n['t'], n['s'], n['v'], n['st']

                @blk.sync
                def _(sy):
                    for q in range(NQ):
                        if q >= 3:
                            sy.wait_ge(s_t, T0 + 2 * (q - 2))
                        for gg in range(2):
                            sy.dma_start(ftile[q % 3][:, 4 * gg:4 * (gg + 1), :],
                                         fsrc[:, 4 * gg:4 * (gg + 1), 128 * q:128 * (q + 1)]).then_inc(s_ld, 16)
                            n['ld'] += 1

                @blk.gpsimd
                def _(g_):
                    for q in range(NQ):
                        if q >= 3:
                            g_.wait_ge(s_t, T0 + 2 * (q - 2))
                        for gg in range(2, 4):
                            g_.dma_start(ftile[q % 3][:, 4 * gg:4 * (gg + 1), :],
                                         fsrc[:, 4 * gg:4 * (gg + 1), 128 * q:128 * (q + 1)]).then_inc(s_st, 16)
                            n['st'] += 1

                @blk.tensor
                def _(t):
                    t.wait_ge(s_ld, 16 * (L0_ + 2))
                    t.wait_ge(s_st, 16 * (ST4_ + 2))
                    t.drain()
                    for q in range(NQ):
                        t.wait_ge(s_ld, 16 * (L0_ + min(2 * (q + 2), 2 * NQ)))
                        t.wait_ge(s_st, 16 * (ST4_ + min(2 * (q + 2), 2 * NQ)))
                        if q >= 3:
                            t.wait_ge(s_v, V0 + 2 * ((q - 3) // 2 + 1))
                        pv = psV[q % 3]
                        for j in range(NT):
                            for i in range(2):
                                ins = t.matmul(pv[:, 512 * i:512 * (i + 1)],
                                               ftile[q % 3][:, j, :],
                                               vbuf[:, j, 512 * i:512 * (i + 1)],
                                               start=(j == 0), stop=(j == NT - 1))
                        ins.then_inc(s_t, 1)
                        n['t'] += 1
                        if q >= 1:
                            t.wait_ge(s_s, S0 + q)
                        for i in range(2):
                            ins = t.matmul(psH[:, 512 * i:512 * (i + 1)],
                                           G1[:, 128 * q:128 * (q + 1)],
                                           W3aug[:, 512 * i:512 * (i + 1)],
                                           start=True, stop=True)
                        ins.then_inc(s_t, 1)
                        n['t'] += 1

                @blk.scalar
                def _(s):
                    for q in range(NQ):
                        s.wait_ge(s_t, T0 + 2 * (q + 1))
                        if q >= 2:
                            s.wait_ge(s_v, V0 + 2 * ((q - 2) // 2 + 1))
                        s.activation(Hsb[q % 2][:], psH[:], AF.Copy).then_inc(s_s, 1)
                        n['s'] += 1

                @blk.vector
                def _(v):
                    for m in range(16):
                        qre, qim = 2 * m, 2 * m + 1
                        v.wait_ge(s_t, T0 + 2 * qim + 1)
                        v.wait_ge(s_s, S0 + qim + 1)
                        re, im = psV[qre % 3], psV[qim % 3]
                        v.tensor_tensor(tp1[:], re[:], Hsb[0][:], ALU.mult)
                        v.tensor_tensor(tp2[:], im[:], Hsb[1][:], ALU.mult)
                        v.tensor_sub(Yspec[:, qre, :], tp1[:], tp2[:]).then_inc(s_v, 1)
                        v.tensor_tensor(tp1[:], re[:], Hsb[1][:], ALU.mult)
                        v.tensor_tensor(tp2[:], im[:], Hsb[0][:], ALU.mult)
                        v.tensor_add(Yspec[:, qim, :], tp1[:], tp2[:]).then_inc(s_v, 1)
                        n['v'] += 2
                    v.wait_ge(s_t, T0 + 2 * 32 + 1)
                    v.wait_ge(s_s, S0 + 33)
                    v.tensor_tensor(Yspec[:, 32, :], psV[32 % 3][:], Hsb[0][:],
                                    ALU.mult).then_inc(s_v, 2)
                    n['v'] += 2

        with contextlib.ExitStack() as ps5:
            itile = [ps5.enter_context(nc.sbuf_tensor(f"itile{i}", [128, NQ, 128], BF16)) for i in range(3)]
            psC = [ps5.enter_context(nc.psum_tensor(f"psC{i}", [128, D], F32)) for i in range(2)]
            isrc = invt_d.ap().transpose([1, 0, 2])
            with nc.Block() as blk:
                L0_, T0, V0, ST5_ = n['ld'], n['t'], n['v'], n['st']

                @blk.sync
                def _(sy):
                    for j in range(NT):
                        if j >= 3:
                            sy.wait_ge(s_t, T0 + j - 2)
                        for gg in range(5):
                            q0, q1 = 4 * gg, min(4 * (gg + 1), NQ)
                            sy.dma_start(itile[j % 3][:, q0:q1, :],
                                         isrc[:, q0:q1, 128 * j:128 * (j + 1)]).then_inc(s_ld, 16)
                            n['ld'] += 1

                @blk.gpsimd
                def _(g_):
                    for j in range(NT):
                        if j >= 3:
                            g_.wait_ge(s_t, T0 + j - 2)
                        for gg in range(5, 9):
                            q0, q1 = 4 * gg, min(4 * (gg + 1), NQ)
                            g_.dma_start(itile[j % 3][:, q0:q1, :],
                                         isrc[:, q0:q1, 128 * j:128 * (j + 1)]).then_inc(s_st, 16)
                            n['st'] += 1

                @blk.tensor
                def _(t):
                    t.wait_ge(s_ld, 16 * (L0_ + 5))
                    t.wait_ge(s_st, 16 * (ST5_ + 4))
                    t.drain()
                    for j in range(NT):
                        t.wait_ge(s_ld, 16 * (L0_ + min(5 * (j + 2), 5 * NT)))
                        t.wait_ge(s_st, 16 * (ST5_ + min(4 * (j + 2), 4 * NT)))
                        if j >= 2:
                            t.wait_ge(s_v, V0 + j - 1)
                        pc = psC[j % 2]
                        for q in range(NQ):
                            for i in range(2):
                                ins = t.matmul(pc[:, 512 * i:512 * (i + 1)],
                                               itile[j % 3][:, q, :],
                                               Yspec[:, q, 512 * i:512 * (i + 1)],
                                               start=(q == 0), stop=(q == NQ - 1))
                        ins.then_inc(s_t, 1)
                        n['t'] += 1

                @blk.vector
                def _(v):
                    for j in range(NT):
                        v.wait_ge(s_t, T0 + j + 1)
                        v.tensor_tensor(vbuf[:, j, :], sbf[:, j, :],
                                        psC[j % 2][:], ALU.mult).then_inc(s_v, 1)
                        n['v'] += 1

        if debug:
            with nc.Block() as blk:
                @blk.gpsimd
                def _(g):
                    for nm, src in [("dbg_xnT", xnT[:, :, :]), ("dbg_G", G1[:, :]),
                                    ("dbg_Faug", Faug[:, :, :]), ("dbg_u", sbf[:, :, :]),
                                    ("dbg_Y", Yspec[:, :, :]), ("dbg_gated", vbuf[:, :, :])]:
                        g.dma_start(dbg[nm].ap(), src).then_inc(s_st, 16)
                        n['st'] += 1
                    g.wait_ge(s_st, 16 * n['st'])

    if max_phase < 6:
        return finish()

    with contextlib.ExitStack() as ps6:
        wof = [ps6.enter_context(nc.sbuf_tensor(f"wof{i}", [128, D], F32)) for i in range(2)]
        wobf = ps6.enter_context(nc.sbuf_tensor("wobf", [128, NCH, D], BF16))
        gstg = [ps6.enter_context(nc.sbuf_tensor(f"gstg{i}", [128, NCH, 128], BF16)) for i in range(2)]
        glhs = [ps6.enter_context(nc.sbuf_tensor(f"glhs{i}", [128, NCH, 128], BF16)) for i in range(2)]
        kuaB = ps6.enter_context(nc.sbuf_tensor("kuaB", [128, 2, 4, 1024], U8))
        kwoB = ps6.enter_context(nc.sbuf_tensor("kwoB", [128, 2, 4, 1024], BF16))
        psS = [ps6.enter_context(nc.psum_tensor(f"psS{i}", [128, D], F32)) for i in range(2)]
        psT6 = [ps6.enter_context(nc.psum_tensor(f"psT6{i}", [128, 128], BF16)) for i in range(4)]
        with nc.Block() as blk:
            L0_, S0 = n['ld'], n['s']

            @blk.sync
            def _(sy):
                for cc in range(NCH):
                    if cc >= 2:
                        sy.wait_ge(s_s, S0 + cc - 1)
                    sy.dma_start(wof[cc % 2][:], Wo_d[128 * cc:128 * (cc + 1), :]).then_inc(s_ld, 16)
                    n['ld'] += 1

            @blk.scalar
            def _(s):
                for cc in range(NCH):
                    s.wait_ge(s_ld, 16 * (L0_ + cc + 1))
                    s.activation(wobf[:, cc, :], wof[cc % 2][:], AF.Copy).then_inc(s_s, 1)
                    n['s'] += 1

        kjobs = ([(k1_d, k1bf_d, g, 4 * rb) for g in range(4) for rb in range(NJ // 4)]
                 + [(k2_d, k2bf_d, 0, 4 * rb) for rb in range(NJ // 4)])
        kbatch = [list(range(3 * j, min(3 * j + 3, len(kjobs)))) for j in range(NT)]
        with nc.Block() as blk:
            T0, V0, S0, ST0 = n['t'], n['v'], n['s'], n['st']
            K0_, K20, KC0 = n['k'], n['k2'], n['kc']
            tr_inc = {}
            wo_done = {}

            @blk.sync
            def _(sy):
                for j in range(NT):
                    for i in kbatch[j]:
                        src_d, dst_d, g, r0 = kjobs[i]
                        if i >= 2:
                            sy.wait_ge(s_kc, KC0 + i - 1)
                        sy.dma_start(kuaB[:, i % 2, :, :],
                                     src_d.ap().transpose([2, 0, 1, 3])[:, g, r0:r0 + 4, :]).then_inc(s_k, 16)
                        n['k'] += 1

            @blk.tensor
            def _(t):
                for j in range(NT + 1):
                    if j < NT:
                        for cc in range(NCH):
                            k = j * NCH + cc
                            if k >= 4:
                                t.wait_ge(s_v, V0 + k - 3)
                            t.transpose(psT6[k % 4][:], vbuf[:, j, 128 * cc:128 * (cc + 1)],
                                        eye_s[:]).then_inc(s_t, 1)
                            n['t'] += 1
                            tr_inc[(j, cc)] = n['t']
                        if j == NT - 1:
                            for d in range(2):
                                k = NT * NCH + d
                                t.wait_ge(s_v, V0 + k - 3)
                                t.transpose(psT6[k % 4][:], eye_s[:], eye_s[:]).then_inc(s_t, 1)
                                n['t'] += 1
                                tr_inc[(NT, d)] = n['t']
                    if j >= 1:
                        jw = j - 1
                        pp = psS[jw % 2]
                        if jw >= 2:
                            t.wait_ge(s_s, S0 + jw - 1)
                        t.wait_ge(s_st, 16 * (ST0 + jw + 1))
                        for cc in range(NCH):
                            for i in range(2):
                                ins = t.matmul(pp[:, 512 * i:512 * (i + 1)],
                                               glhs[jw % 2][:, cc, :],
                                               wobf[:, cc, 512 * i:512 * (i + 1)],
                                               start=(cc == 0), stop=False)
                            ins.then_inc(s_t, 1)
                            n['t'] += 1
                        for i in range(2):
                            ins = t.matmul(pp[:, 512 * i:512 * (i + 1)], ones_r[:],
                                           bo_bf[:, 512 * i:512 * (i + 1)],
                                           start=False, stop=True)
                        ins.then_inc(s_t, 1)
                        n['t'] += 1
                        wo_done[jw] = n['t']

            @blk.vector
            def _(v):
                def tr2(j, cc):
                    k = j * NCH + cc + 2
                    return tr_inc[(k // NCH, k % NCH)] if k < NT * NCH else tr_inc[(NT, k - NT * NCH)]
                for j in range(NT):
                    for cc in range(NCH):
                        v.wait_ge(s_t, tr2(j, cc))
                        if j >= 2:
                            v.wait_ge(s_st, 16 * (ST0 + j - 1))
                        v.tensor_copy(gstg[j % 2][:, cc, :], psT6[(j * NCH + cc) % 4][:]).then_inc(s_v, 1)
                        n['v'] += 1

            @blk.gpsimd
            def _(g):
                for j in range(NT):
                    g.wait_ge(s_v, V0 + (j + 1) * NCH)
                    g.dma_start(glhs[j % 2].ap(), gstg[j % 2].ap()).then_inc(s_st, 16)
                    n['st'] += 1
                    for i in (kbatch[j - 1] if j >= 1 else []):
                        src_d, dst_d, g_bl, r0 = kjobs[i]
                        g.wait_ge(s_kc, KC0 + i + 1)
                        g.dma_start(dst_d.ap().transpose([2, 0, 1, 3])[:, g_bl, r0:r0 + 4, :],
                                    kwoB[:, i % 2, :, :]).then_inc(s_k2, 16)
                        n['k2'] += 1
                g.wait_ge(s_k2, 16 * n['k2'])

            @blk.scalar
            def _(s):
                for j in range(NT):
                    s.wait_ge(s_t, wo_done[j])
                    s.activation(sbf[:, j, :], psS[j % 2][:], AF.Copy).then_inc(s_s, 1)
                    n['s'] += 1
                    for i in kbatch[j]:
                        s.wait_ge(s_k, 16 * (K0_ + i + 1))
                        if i >= 2:
                            s.wait_ge(s_k2, 16 * (K20 + i - 1))
                        s.activation(kwoB[:, i % 2, :, :], kuaB[:, i % 2, :, :],
                                     AF.Copy, scale=2.0, bias=-1.0).then_inc(s_kc, 1)
                        n['kc'] += 1

    if debug:
        with nc.Block() as blk:
            @blk.gpsimd
            def _(g):
                g.dma_start(dbg["dbg_s"].ap(), sbf[:, :, :]).then_inc(s_st, 16)
                n['st'] += 1
                for nm, srcd in [("dbg_m0", m0_d), ("dbg_k1b", k1bf_d), ("dbg_k2b", k2bf_d)]:
                    for gi in range(srcd.shape[0]):
                        g.dma_start(dbg[nm].ap()[gi], srcd.ap()[gi]).then_inc(s_st, 16)
                        n['st'] += 1
                g.wait_ge(s_st, 16 * n['st'])

    if max_phase < 7:
        return finish()

    es_mid.close()
    with contextlib.ExitStack() as ps7:
        h0 = ps7.enter_context(nc.sbuf_tensor("h0", [128, NJ, B], BF16))
        h1 = ps7.enter_context(nc.sbuf_tensor("h1", [128, NJ, B], BF16))
        gT = ps7.enter_context(nc.sbuf_tensor("gT", [128, NCH, B], BF16))
        wsl = [ps7.enter_context(nc.sbuf_tensor(f"wsl{i}", [128, 4, 1024], BF16)) for i in range(4)]
        gch = ps7.enter_context(nc.sbuf_tensor("gch", [128, D], BF16))
        xch7 = [ps7.enter_context(nc.sbuf_tensor(f"xch7{i}", [128, D], F32)) for i in range(2)]
        och = [ps7.enter_context(nc.sbuf_tensor(f"och{i}", [128, D], F32)) for i in range(2)]
        sgs = ps7.enter_context(nc.sbuf_tensor("sgs", [128, D], F32))

        for tb in range(NB):
            tok = slice(tb * B, (tb + 1) * B)
            psb = contextlib.ExitStack()
            psL = [psb.enter_context(nc.psum_tensor(f"psL{tb}_{i}", [128, B], F32))
                   for i in range(8)]
            with nc.Block() as blk:
                L0_, T0, S0 = n['ld'], n['t'], n['s']
                kw_barrier = n['st']
                NRB0 = NCH // 4

                @blk.sync
                def _(sy):
                    sy.wait_ge(s_st, 16 * kw_barrier)
                    for g in range(4):
                        for rb in range(NRB0):
                            k = g * NRB0 + rb
                            if k >= 4:
                                sy.wait_ge(s_t, T0 + 4 * (k - 3))
                            sy.dma_start(wsl[k % 4].ap(),
                                         m0_d.ap().transpose([2, 0, 1, 3])[:, g, 4 * rb:4 * rb + 4, :]).then_inc(s_ld, 16)
                            n['ld'] += 1

                @blk.tensor
                def _(t):
                    t.wait_ge(s_ld, 16 * (L0_ + 1))
                    t.drain()
                    for g in range(4):
                        if g >= 1:
                            t.wait_ge(s_s, S0 + 8 * g)
                        for rb in range(NRB0):
                            k = g * NRB0 + rb
                            t.wait_ge(s_ld, 16 * (L0_ + min(k + 2, 4 * NRB0)))
                            for ri in range(4):
                                f = 4 * rb + ri
                                for jj in range(8):
                                    ins = t.matmul(psL[jj][:],
                                                   wsl[k % 4][:, ri, 128 * jj:128 * (jj + 1)],
                                                   xnT[:, f, tok], start=(f == 0),
                                                   stop=(f == NCH - 1))
                                ins.then_inc(s_t, 1)
                                n['t'] += 1

                @blk.scalar
                def _(s):
                    for g in range(4):
                        s.wait_ge(s_t, T0 + NCH * (g + 1))
                        for jj in range(4):
                            s.activation(h0[:, 8 * g + jj, :], psL[jj][:],
                                         AF.Relu, scale=sc2[:, 0:1]).then_inc(s_s, 1)
                            n['s'] += 1

                @blk.vector
                def _(v):
                    for g in range(4):
                        v.wait_ge(s_t, T0 + NCH * (g + 1))
                        for jj in range(4, 8):
                            v.tensor_scalar(h0[:, 8 * g + jj, :], psL[jj][:], sc2[:, 0:1],
                                            0.0, ALU.mult, ALU.max).then_inc(s_s, 1)
                            n['s'] += 1

            with nc.Block() as blk:
                L0_, T0, S0 = n['ld'], n['t'], n['s']
                NRB1 = NJ // 4

                @blk.sync
                def _(sy):
                    for g in range(4):
                        for rb in range(NRB1):
                            k = g * NRB1 + rb
                            if k >= 4:
                                sy.wait_ge(s_t, T0 + 4 * (k - 3))
                            sy.dma_start(wsl[k % 4].ap(),
                                         k1bf_d.ap().transpose([2, 0, 1, 3])[:, g, 4 * rb:4 * rb + 4, :]).then_inc(s_ld, 16)
                            n['ld'] += 1

                @blk.tensor
                def _(t):
                    t.wait_ge(s_ld, 16 * (L0_ + 1))
                    t.drain()
                    for g in range(4):
                        if g >= 1:
                            t.wait_ge(s_s, S0 + 8 * g)
                        for rb in range(NRB1):
                            k = g * NRB1 + rb
                            t.wait_ge(s_ld, 16 * (L0_ + min(k + 2, 4 * NRB1)))
                            for ri in range(4):
                                j1 = 4 * rb + ri
                                for jj in range(8):
                                    ins = t.matmul(psL[jj][:],
                                                   wsl[k % 4][:, ri, 128 * jj:128 * (jj + 1)],
                                                   h0[:, j1, :], start=(j1 == 0),
                                                   stop=(j1 == NJ - 1))
                                ins.then_inc(s_t, 1)
                                n['t'] += 1

                @blk.scalar
                def _(s):
                    for g in range(4):
                        s.wait_ge(s_t, T0 + NJ * (g + 1))
                        for jj in range(4):
                            s.activation(h1[:, 8 * g + jj, :], psL[jj][:],
                                         AF.Relu, scale=sc2[:, 1:2]).then_inc(s_s, 1)
                            n['s'] += 1

                @blk.vector
                def _(v):
                    for g in range(4):
                        v.wait_ge(s_t, T0 + NJ * (g + 1))
                        for jj in range(4, 8):
                            v.tensor_scalar(h1[:, 8 * g + jj, :], psL[jj][:], sc2[:, 1:2],
                                            0.0, ALU.mult, ALU.max).then_inc(s_s, 1)
                            n['s'] += 1

            with nc.Block() as blk:
                L0_, T0, S0 = n['ld'], n['t'], n['s']
                NRB2 = NJ // 4

                @blk.sync
                def _(sy):
                    for rb in range(NRB2):
                        k = rb
                        if k >= 4:
                            sy.wait_ge(s_t, T0 + 4 * (k - 3))
                        sy.dma_start(wsl[k % 4].ap(),
                                     k2bf_d.ap().transpose([2, 0, 1, 3])[:, 0, 4 * rb:4 * rb + 4, :]).then_inc(s_ld, 16)
                        n['ld'] += 1

                @blk.tensor
                def _(t):
                    t.wait_ge(s_ld, 16 * (L0_ + 1))
                    t.drain()
                    for rb in range(NRB2):
                        k = rb
                        t.wait_ge(s_ld, 16 * (L0_ + min(k + 2, NRB2)))
                        for ri in range(4):
                            j1 = 4 * rb + ri
                            for jj in range(8):
                                ins = t.matmul(psL[jj][:],
                                               wsl[k % 4][:, ri, 128 * jj:128 * (jj + 1)],
                                               h1[:, j1, :], start=(j1 == 0),
                                               stop=(j1 == NJ - 1))
                            ins.then_inc(s_t, 1)
                            n['t'] += 1

                @blk.scalar
                def _(s):
                    s.wait_ge(s_t, T0 + NJ)
                    for jj in range(8):
                        s.activation(gT[:, jj, :], psL[jj][:],
                                     AF.Silu, scale=sc2[:, 2:3]).then_inc(s_s, 1)
                        n['s'] += 1

            psb.close()
            psb2 = contextlib.ExitStack()
            psT7 = [psb2.enter_context(nc.psum_tensor(f"psT7{tb}_{i}", [128, 128], BF16))
                    for i in range(4)]
            with nc.Block() as blk:
                L0_, V0, T0, ST0 = n['ld'], n['v'], n['t'], n['st']

                @blk.sync
                def _(sy):
                    for ts_ in range(4):
                        j = tb * 4 + ts_
                        if ts_ >= 2:
                            sy.wait_ge(s_v, V0 + 9 * (ts_ - 1))
                        sy.dma_start(xch7[ts_ % 2][:], x_d[128 * j:128 * (j + 1), :]).then_inc(s_ld, 16)
                        n['ld'] += 1

                @blk.tensor
                def _(t):
                    k = 0
                    for ts_ in range(4):
                        for cc in range(NCH):
                            if k >= 4:
                                t.wait_ge(s_v, V0 + (k - 4) + (k - 4) // NCH + 1)
                            t.transpose(psT7[k % 4][:], gT[:, cc, 128 * ts_:128 * (ts_ + 1)],
                                        eye_s[:]).then_inc(s_t, 1)
                            n['t'] += 1
                            k += 1
                    for d in range(2):
                        t.wait_ge(s_v, V0 + (k - 4) + (k - 4) // NCH + 1)
                        t.transpose(psT7[k % 4][:], eye_s[:], eye_s[:]).then_inc(s_t, 1)
                        n['t'] += 1
                        k += 1

                @blk.vector
                def _(v):
                    k = 0
                    for ts_ in range(4):
                        j = tb * 4 + ts_
                        for cc in range(NCH):
                            v.wait_ge(s_t, T0 + k + 3)
                            v.tensor_copy(gch[:, 128 * cc:128 * (cc + 1)],
                                          psT7[k % 4][:]).then_inc(s_v, 1)
                            n['v'] += 1
                            k += 1
                        v.tensor_tensor(sgs[:], sbf[:, j, :], gch[:], ALU.mult)
                        v.wait_ge(s_ld, 16 * (L0_ + ts_ + 1))
                        if ts_ >= 2:
                            v.wait_ge(s_st, 16 * (ST0 + ts_ - 1))
                        v.tensor_add(och[ts_ % 2][:], xch7[ts_ % 2][:], sgs[:]).then_inc(s_v, 1)
                        n['v'] += 1

                @blk.gpsimd
                def _(g):
                    for ts_ in range(4):
                        j = tb * 4 + ts_
                        g.wait_ge(s_v, V0 + 9 * (ts_ + 1))
                        g.dma_start(out_d[128 * j:128 * (j + 1), :], och[ts_ % 2][:]).then_inc(s_st, 16)
                        n['st'] += 1
            psb2.close()

    return finish()


_NC_CACHE = {}


def _get_nc(debug=False):
    key = bool(debug)
    if key not in _NC_CACHE:
        _NC_CACHE[key] = build_nc(debug=key)
    return _NC_CACHE[key]


def _in_maps(inputs):
    hc = _host_consts()
    sc = np.empty((128, 3), np.float32)
    sc[:, 0] = np.float32(inputs['L0_ffn_s0'])
    sc[:, 1] = np.float32(inputs['L0_ffn_s1'])
    sc[:, 2] = np.float32(inputs['L0_ffn_s2'])
    shared = {
        "Wu": np.ascontiguousarray(inputs['L0_attn_Wu'], np.float32),
        "Wv": np.ascontiguousarray(inputs['L0_attn_Wv'], np.float32),
        "Wo": np.ascontiguousarray(inputs['L0_attn_Wo'], np.float32),
        "bu": np.ascontiguousarray(np.asarray(inputs['L0_attn_bu']).reshape(1, D), np.float32),
        "bv": np.ascontiguousarray(np.asarray(inputs['L0_attn_bv']).reshape(1, D), np.float32),
        "bo": np.ascontiguousarray(np.asarray(inputs['L0_attn_bo']).reshape(1, D), np.float32),
        "W1": np.ascontiguousarray(inputs['L0_attn_W1'], np.float32),
        "W2": np.ascontiguousarray(inputs['L0_attn_W2'], np.float32),
        "W3": np.ascontiguousarray(inputs['L0_attn_W3'], np.float32),
        "b1": np.ascontiguousarray(np.asarray(inputs['L0_attn_b1']).reshape(64, 1), np.float32),
        "b2": np.ascontiguousarray(np.asarray(inputs['L0_attn_b2']).reshape(64, 1), np.float32),
        "b3": np.ascontiguousarray(np.asarray(inputs['L0_attn_b3']).reshape(1, D), np.float32),
        "k0a_b": np.ascontiguousarray(
            np.asarray(inputs['L0_ffn_k0']).view(np.uint8)[:D].reshape(NCH, 128, 4, 1024).transpose(2, 0, 1, 3)),
        "k0b_b": np.ascontiguousarray(
            np.asarray(inputs['L0_ffn_k0']).view(np.uint8)[D:].reshape(NCH, 128, 4, 1024).transpose(2, 0, 1, 3)),
        "k1_b": np.ascontiguousarray(
            np.asarray(inputs['L0_ffn_k1']).view(np.uint8).reshape(NJ, 128, 4, 1024).transpose(2, 0, 1, 3)),
        "k2_b": np.ascontiguousarray(
            np.asarray(inputs['L0_ffn_k2']).view(np.uint8).reshape(NJ, 128, 1, 1024).transpose(2, 0, 1, 3)),
        "sc": sc,
        "peT": hc['peT'],
        "dec": hc['dec'],
        "invcoef": hc['invcoef'],
        "eye": hc['eye'],
    }
    xs = np.ascontiguousarray(np.asarray(inputs['L0_attn_x']), np.float32)
    return [dict(shared, x=np.ascontiguousarray(xs[b])) for b in range(8)]


def kernel(**inputs):
    in_maps = _in_maps(inputs)
    nc = _get_nc(debug=False)
    res = run_bass_kernel_spmd(nc, in_maps, core_ids=list(range(8)))
    return np.stack([np.asarray(res.results[b]["out"]) for b in range(8)], axis=0)
